# version 05
# baseline (speedup 1.0000x reference)
import os

import numpy as np

import concourse.bass as bass
import concourse.mybir as mybir
from concourse.bass_utils import run_bass_kernel_spmd

F32 = mybir.dt.float32
BF16 = mybir.dt.bfloat16
I32 = mybir.dt.int32
ALU = mybir.AluOpType
ACT = mybir.ActivationFunctionType

DM = 1024
DI = 2048
DH = DI // 2
NS = 16
SEQ = 8192
TPC = 4096
TT = 512
NTILE = TPC // TT
TC = 1024
NCHUNK = SEQ // TC
KDM = DM // 128
KDI = DI // 128
MI = 4096
KMI = MI // 128
NDB = DH // 128
EPS_LN = 1e-5
EPS_RMS = float(np.finfo(np.float32).eps)
GROUPS = [[0, 1], [2, 3], [4, 5], [6, 7]]
ENGINES = ("tensor", "vector", "scalar", "gpsimd", "sync")
_LAST_SCH = [None]

MSK_MW = 0
MSK_YL = 24
MSK_YU = 28
MSK_XH = 32
NMSK = 40


class Sch:
    def __init__(self):
        self.ops = {e: [] for e in ENGINES}
        self.nseq = {e: 0 for e in ENGINES}
        self.ndma = {e: 0 for e in ENGINES}
        self.awaited = {e: set() for e in ENGINES}
        self.snapshot_fn = None
        self.restore_fn = None

    def op(self, engine, emit, deps=(), is_dma=False, tag=""):
        myseq = self.nseq[engine]
        dd = []
        same_max = -1
        for d in deps:
            if d is None:
                continue
            (de, ds, ddma) = d
            if ddma == "ALL":
                key = (de, ds, "ALL")
                if key not in dd:
                    dd.append(key)
                continue
            if de == engine and not ddma:
                same_max = max(same_max, ds)
                continue
            key = (de, ds, ddma)
            if key not in dd:
                dd.append(key)
            if not ddma:
                self.awaited[de].add(ds)
        self.ops[engine].append(dict(
            emit=emit, deps=dd, seq=myseq, is_dma=is_dma, tag=tag,
            same_max=same_max,
            snap=self.snapshot_fn() if self.snapshot_fn else None))
        self.nseq[engine] += 1
        if is_dma:
            self.ndma[engine] += 1
            return (engine, self.ndma[engine], True)
        return (engine, myseq, False)

    def last(self, engine):
        if self.nseq[engine] == 0:
            return None
        rec = self.ops[engine][-1]
        if rec["is_dma"]:
            return (engine, self.ndma[engine], True)
        return (engine, rec["seq"], False)

    def barrier(self):
        lasts = [self.last(e) for e in ENGINES]
        alldma = [(e2, self.ndma[e2], "ALL") for e2 in ENGINES if self.ndma[e2]]
        toks = []
        for e in ENGINES:
            deps = [t for t in lasts if t is not None and t[0] != e and not t[2]]
            deps = deps + alldma
            toks.append(self.op(e, lambda eng: eng.nop(), deps, tag="barrier"))
        return toks

    def finalize(self, block, sems, dma_sems, npool):
        incval = {}
        for e in ENGINES:
            aw = sorted(self.awaited[e])
            incval[e] = {s: i + 1 for i, s in enumerate(aw)}

        def dma_sem_val(de, di):
            return dma_sems[de][(di - 1) % npool], 16 * ((di - 1) // npool + 1)

        def run_engine(eng_obj, e):
            hi = {}
            ndma_issued = 0
            drained_upto = -1
            for rec in self.ops[e]:
                if rec["same_max"] > drained_upto:
                    eng_obj.drain()
                    drained_upto = rec["seq"] - 1
                for (de, ds, ddma) in rec["deps"]:
                    if ddma == "ALL":
                        for s in range(min(ds, npool)):
                            sem = dma_sems[de][s]
                            val = 16 * ((ds - 1 - s) // npool + 1)
                            k = id(sem)
                            if hi.get(k, -1) >= val:
                                continue
                            eng_obj.wait_ge(sem, val)
                            hi[k] = val
                        continue
                    if ddma:
                        sem, val = dma_sem_val(de, ds)
                    else:
                        sem, val = sems[de], incval[de][ds]
                    k = id(sem)
                    if hi.get(k, -1) >= val:
                        continue
                    eng_obj.wait_ge(sem, val)
                    hi[k] = val
                if rec["is_dma"]:
                    di = ndma_issued + 1
                    sem, val = dma_sem_val(e, di)
                    if di > npool:
                        pv = 16 * ((di - 1) // npool)
                        k = id(sem)
                        if hi.get(k, -1) < pv:
                            eng_obj.wait_ge(sem, pv)
                            hi[k] = pv
                    if rec["snap"] is not None:
                        self.restore_fn(rec["snap"])
                    ins = rec["emit"](eng_obj)
                    ins.then_inc(sem, 16)
                    ndma_issued += 1
                else:
                    if rec["snap"] is not None:
                        self.restore_fn(rec["snap"])
                    ins = rec["emit"](eng_obj)
                    if rec["seq"] in incval[e]:
                        if ins is None:
                            raise RuntimeError("awaited op returned None: " + rec["tag"])
                        ins.then_inc(sems[e], 1)

        for e in ENGINES:
            getattr(block, e)(lambda eng, e=e: run_engine(eng, e))


class RT:
    __slots__ = ("w", "r")

    def __init__(self):
        self.w = None
        self.r = []


class Regs:
    def __init__(self):
        self.d = {}

    def __getitem__(self, k):
        if k not in self.d:
            self.d[k] = RT()
        return self.d[k]

    def rdep(self, *ks):
        return [self[k].w for k in ks if self[k].w is not None]

    def rd(self, tok, *ks):
        for k in ks:
            self[k].r.append(tok)

    def wdep(self, *ks):
        out = []
        for k in ks:
            r = self[k]
            if r.w is not None:
                out.append(r.w)
            out.extend(r.r)
        return out

    def wr(self, tok, *ks):
        for k in ks:
            r = self[k]
            r.w = tok
            r.r = []


A_WX = 0
A_XT = 16384
A_XN = 20480
A_SQ = 24576
A_U = 28672
A_CIN = 29696
A_ZE = 30736
A_XD = 31760
A_DT = 32272
A_WW = 33808
A_TK = 34320
RT_KF = 34368
RT_D = 34880
FA_N = 44032
C_MW = 0
C_ACC = 16384
C_A = 24576
C_B = 28672
C_H = 32768
C_BC = 36864
C_SEL = 37888
C_G = 4
S0_W0 = 0
S0_W1 = 4096
S0_XN = 12800
B_XN = 0
B_T0 = 4096
B_SB5 = 7168
B_ABS = 7680
B_RT = 11776
FB_N = 19456
D_WO = 0
D_YG = 16384
D_T0 = 24576


def build_program(dev=()):
    nc = bass.Bass("TRN2", target_bir_lowering=False)
    sch = Sch()
    _LAST_SCH[0] = sch
    dev = set(dev)
    R = Regs()

    def din(name, shape, dtype=F32):
        return nc.dram_tensor(name, list(shape), dtype, kind="ExternalInput").ap()

    def dscr(name, shape, dtype=F32):
        kind = "ExternalOutput" if name in dev else "Internal"
        return nc.dram_tensor(name, list(shape), dtype, kind=kind).ap()

    x_ext = din("x", (DM, TPC + 3))
    LI = [{}, {}]
    for i in range(2):
        LI[i]["w_in_x"] = din(f"l{i}_w_in_x", (DM, DI))
        LI[i]["w_in_z"] = din(f"l{i}_w_in_z", (DM, DI))
        LI[i]["conv_w"] = din(f"l{i}_conv_w", (DI, 4))
        LI[i]["conv_b"] = din(f"l{i}_conv_b", (DI,))
        LI[i]["a_log"] = din(f"l{i}_a_log", (DI, NS))
        LI[i]["d"] = din(f"l{i}_d", (DI,))
        LI[i]["w_out"] = din(f"l{i}_w_out", (DI, DM))
        LI[i]["b_dt"] = din(f"l{i}_b_dt", (DI,))
    LI[0]["w_xproj"] = din("l0_w_xproj", (DI, 33))
    LI[0]["w_dt"] = din("l0_w_dt", (DI,))
    LI[0]["norm_w"] = din("l0_norm_w", (DM,))
    LI[0]["wg"] = din("l0_w_gate", (DM, MI))
    LI[0]["wu"] = din("l0_w_up", (DM, MI))
    LI[0]["wd"] = din("l0_w_down", (MI, DM))
    LI[1]["w_xproj"] = din("l1_w_xproj", (DI, 96))
    LI[1]["w_dtp"] = din("l1_w_dtproj", (64, DI))
    LI[1]["w1"] = din("l1_w1", (DM, MI))
    LI[1]["b1"] = din("l1_b1", (MI,))
    LI[1]["w2"] = din("l1_w2", (MI, DM))
    LI[1]["b2"] = din("l1_b2", (DM,))
    msk_in = din("msk", (128, NMSK))
    eye_in = din("eye", (128, 128))
    seln_in = din("seln", (32, 32 * 128))
    y_ext = nc.dram_tensor("y", [DM, TPC], F32, kind="ExternalOutput").ap()

    xn_s = dscr("xn_s", (DM, TPC))
    xn2_s = dscr("xn2_s", (DM, TPC))
    u_s = dscr("u_s", (DI, TPC))
    z_s = dscr("z_s", (DI, TPC))
    g_s = dscr("g_s", (DM, TPC))
    sg_s = dscr("sg_s", (MI, TPC))
    gu_s = dscr("gu_s", (MI, TPC))
    xq_s = dscr("xq_s", (DM, TPC), BF16)
    srow_s = dscr("srow_s", (4, TPC))
    h1hi_s = dscr("h1hi_s", (MI, TPC), BF16)
    h1lo_s = dscr("h1lo_s", (MI, TPC), BF16)
    x1_s = dscr("x1_s", (DM, TPC + 3))
    m_loc = dscr("m_loc", (DH, TPC))
    w_loc = dscr("w_loc", (DH, TPC))
    wq_sd = {
        "g": dscr("wqg_s", (DM, MI), BF16),
        "u": dscr("wqu_s", (DM, MI), BF16),
        "d": dscr("wqd_s", (MI, DM), BF16),
        "w1": dscr("wt1_s", (DM, MI), BF16),
        "w2": dscr("wt2_s", (MI, DM), BF16),
    }
    cc = {}
    for i in range(2):
        cc[f"m_in{i}"] = dscr(f"cc_m_in{i}", (DH, TPC))
        cc[f"m_out{i}"] = dscr(f"cc_m_out{i}", (2 * DH, TPC))
        cc[f"w_in{i}"] = dscr(f"cc_w_in{i}", (DH, TPC))
        cc[f"w_out{i}"] = dscr(f"cc_w_out{i}", (2 * DH, TPC))
        cc[f"bc_in{i}"] = dscr(f"cc_bc_in{i}", (32, TPC))
        cc[f"bc_out{i}"] = dscr(f"cc_bc_out{i}", (64, TPC))
        for cch in range(NCHUNK):
            cc[f"y_in{i}_{cch}"] = dscr(f"cc_y_in{i}_{cch}", (DH, TC))
            cc[f"y_out{i}_{cch}"] = dscr(f"cc_y_out{i}_{cch}", (2 * DH, TC))
    cc["xt_in"] = dscr("cc_xt_in", (DM, 3))
    cc["xt_out"] = dscr("cc_xt_out", (2 * DM, 3))

    keep = []

    def sb(name, shape, dtype=F32):
        cm = nc.sbuf_tensor(name, list(shape), dtype)
        h = cm.__enter__()
        keep.append(cm)
        return h

    arena = {"cms": [], "fa": None, "wb": None, "ba": None, "n": 0,
             "rt": (RT_KF, RT_D)}

    def use_big():
        for cm in reversed(arena["cms"]):
            cm.__exit__(None, None, None)
        arena["n"] += 1
        cm = nc.sbuf_tensor(f"FAb{arena['n']}", [128, FA_N], F32)
        arena["cms"] = [cm]
        arena["fa"] = cm.__enter__()
        arena["wb"] = None
        arena["ba"] = None
        arena["rt"] = (RT_KF, RT_D)

    def use_bphase():
        for cm in reversed(arena["cms"]):
            cm.__exit__(None, None, None)
        arena["n"] += 1
        cm1 = nc.sbuf_tensor(f"FBs{arena['n']}", [128, FB_N], F32)
        cm2 = nc.sbuf_tensor(f"WBw{arena['n']}", [128, 32768], BF16)
        cm3 = nc.sbuf_tensor(f"BAq{arena['n']}", [128, 16390], BF16)
        arena["cms"] = [cm1, cm2, cm3]
        arena["fa"] = cm1.__enter__()
        arena["wb"] = cm2.__enter__()
        arena["ba"] = cm3.__enter__()
        arena["rt"] = (B_RT, B_RT + 512)

    sch.snapshot_fn = lambda: (arena["fa"], arena["wb"], arena["ba"],
                               arena["rt"])
    sch.restore_fn = lambda s: arena.update(fa=s[0], wb=s[1], ba=s[2],
                                            rt=s[3])
    IA = sb("IA", (128, 512), I32)
    cons = sb("cons", (128, 512))
    srow = sb("srow", (128, 4352))
    wdtp = sb("wdtp", (64, 2048))
    carry = sb("carry", (128, 128))
    eye = sb("eye_sb", (128, 128))
    cmP = nc.psum_tensor("ps8", [128, 4096], F32)
    ps8 = cmP.__enter__()
    keep.append(cmP)
    use_bphase()

    def FAr(off, n, p0=0, p1=128):
        return arena["fa"].ap()[p0:p1, off:off + n]

    def BAr(off, n, p0=0, p1=128):
        return arena["ba"].ap()[p0:p1, off:off + n]

    def WQv(off, n):
        return arena["wb"].ap()[:, off:off + n]

    def PS(bank, n=512, off=0, p0=0, p1=128):
        return ps8.ap()[p0:p1, bank * 512 + off:bank * 512 + off + n]

    c_ones = 0
    c_onesb = 1
    c_msk = 4
    c_convw = 44
    c_convb = 108
    c_dtw = 124
    c_dtb = 140
    c_dcol = 156
    c_nw = 172
    c_ascale = 180
    c_b1 = 308
    c_b2 = 340
    c_csg = 348
    c_csu = 380
    c_csd = 412
    c_kap = 420

    SEL0 = 0
    R_Q = 128
    R_STAT = 140
    R_PMT = 2188
    R_PMS = 2192
    R_BR = 2320
    R_ALOG = 4096

    def dma(q, dst, src, deps=(), tag="", slow=False):
        return sch.op(q, lambda e, d=dst, s=src, sl=slow: e.dma_start(
            out=d, in_=s, allow_slow_non_contiguous=sl),
            deps, is_dma=True, tag=tag)

    def vec(fn, deps=(), tag=""):
        return sch.op("vector", fn, deps, tag=tag)

    def sca(fn, deps=(), tag=""):
        return sch.op("scalar", fn, deps, tag=tag)

    def ten(fn, deps=(), tag=""):
        return sch.op("tensor", fn, deps, tag=tag)

    def mmg(out_ap, pairs, deps=(), tag="mm"):
        def emit(eng, pairs=tuple(pairs), out_ap=out_ap, n=len(pairs)):
            ins = None
            for i, (l, r) in enumerate(pairs):
                ins = eng.matmul(out_ap, l, r, start=(i == 0), stop=(i == n - 1))
            return ins
        return sch.op("tensor", emit, deps, tag=tag)

    def ag(in_ap, out_ap, deps=(), tag="ag"):
        def emit(eng, a=in_ap, b=out_ap):
            return eng.collective_compute("AllGather", ALU.bypass,
                                          replica_groups=GROUPS, ins=[a], outs=[b])
        return sch.op("gpsimd", emit, deps, tag=tag)

    ONES1 = srow.ap()[0:1, SEL0:SEL0 + 128]
    ONESC = cons.ap()[:, c_ones:c_ones + 1]
    def ONESB():
        return arena["ba"].ap()[:, 16384:16385]

    def mcol(i):
        return cons.ap()[:, c_msk + i:c_msk + i + 1]

    def ccol(c0, n=1):
        return cons.ap()[:, c0:c0 + n]

    t = vec(lambda e: e.memset(ONESC, 1.0), tag="i1")
    R.wr(t, "cons")
    t = vec(lambda e: e.memset(ONESB(), 1.0), tag="i2")
    R.wr(t, "onesb")
    t = vec(lambda e: e.memset(srow.ap()[0:1, SEL0:SEL0 + 128], 1.0), tag="i3")
    R.wr(t, "ones1")
    t = dma("sync", cons.ap()[:, c_msk:c_msk + NMSK], msk_in[:, :], tag="imsk")
    R.wr(t, "msk")
    t = dma("sync", eye.ap()[:, :], eye_in[:, :], tag="ieye")
    R.wr(t, "eye")

    def round_p1(v_ap, out_ap, deps, lo, hi):
        n = v_ap.shape[-1]
        ki = IA.ap()[:, 0:n]
        kf = FAr(arena["rt"][0], n)
        dd2 = FAr(arena["rt"][1], n)
        d0 = list(deps) + R.wdep("rt")
        t1 = vec(lambda e: e.tensor_copy(ki, v_ap), d0, tag="f2i")
        t2 = vec(lambda e: e.tensor_copy(kf, ki), [t1])
        t3 = vec(lambda e: e.tensor_sub(dd2, v_ap, kf), [t2])
        t4 = vec(lambda e: e.scalar_tensor_tensor(kf, dd2, 0.5, kf, ALU.is_gt, ALU.add), [t3])
        t5 = vec(lambda e: e.scalar_tensor_tensor(kf, dd2, -0.5, kf, ALU.is_ge, ALU.add), [t4])
        t6 = vec(lambda e: e.tensor_scalar(out_ap, kf, float(lo), float(hi),
                                           ALU.max, ALU.min), [t5])
        R.wr(t6, "rt")
        return t6

    def pmax_row(src_ap, row_off, deps, tag="pm"):
        toks = []
        tmp = srow.ap()[:, R_PMT:R_PMT + 4]
        d0 = list(deps) + R.wdep("pm") + R.rdep("eye")
        for q in range(4):
            pt = ten(lambda e, q=q: e.transpose(
                PS(7, 128, 0), src_ap[:, q * 128:(q + 1) * 128], eye.ap()[:, :]),
                d0 + toks[-1:], tag=tag + "t")
            toks.append(vec(lambda e, q=q: e.tensor_reduce(
                tmp[:, q:q + 1], PS(7, 128, 0), mybir.AxisListType.X, ALU.max,
                apply_absolute_value=True), [pt], tag=tag + "r"))
        t2 = ten(lambda e: e.transpose(PS(7, 128, 128, 0, 4), tmp, eye.ap()[:, :]),
                 toks, tag=tag + "tb")
        sb4 = srow.ap()[0:4, R_PMS:R_PMS + 128]
        t3 = vec(lambda e: e.tensor_copy(sb4, PS(7, 128, 128, 0, 4)), [t2])
        t4 = dma("sync", srow.ap()[0:1, row_off:row_off + 512], sb4,
                 deps=[t3] + R.wdep("brows"), tag=tag + "d")
        R.wr(t4, "pm")
        R.wr(t4, "brows")
        return t4

    def s0_quant(src, dst, K, M, krow, lo_clip, cs_col, nmblk, tag):
        nK = K // 128
        absacc = srow.ap()[:, R_Q:R_Q + 1]
        tz = vec(lambda e: e.memset(absacc, 0.0), R.rdep("cons"), tag=tag + "z")
        tacc = tz
        for kk in range(nK):
            woff = S0_W0 if kk % 2 == 0 else S0_W1
            wt = FAr(woff, M)
            tl = dma("sync", wt, src[128 * kk:128 * (kk + 1), :],
                     deps=R.wdep(f"qw{kk % 2}"), tag=tag + "ld")
            R.wr(tl, f"qw{kk % 2}")
            tmpc = srow.ap()[:, R_Q + 1:R_Q + 2]
            ta = sca(lambda e, w=wt, t2=tmpc: e.activation(
                out=FAr(S0_XN, M), in_=w, func=ACT.Abs, accum_out=t2),
                [tl, tacc] + R.wdep("qv"), tag=tag + "abs")
            R.rd(ta, f"qw{kk % 2}")
            R.wr(ta, "qv")
            tacc = vec(lambda e, t2=tmpc: e.tensor_add(absacc, absacc, t2), [ta])
        tsum = mmg(PS(4, 1, 0, 0, 1), [(absacc, ONESC)],
                   deps=[tacc] + R.rdep("cons"), tag=tag + "sum")
        kcell = cons.ap()[0:1, c_kap + krow:c_kap + krow + 1]
        kinv_c = srow.ap()[0:1, R_Q + 2:R_Q + 3]
        t1 = vec(lambda e: e.tensor_scalar(kcell, PS(4, 1, 0, 0, 1), 1.0 / (K * M),
                                           lo_clip, ALU.mult, ALU.max), [tsum])
        R.rd(t1, "cons")
        t2 = vec(lambda e: e.reciprocal(kinv_c, kcell), [t1])
        tb = mmg(PS(4, 1, 4), [(ONES1, kinv_c)], deps=[t2] + R.rdep("ones1"),
                 tag=tag + "kb")
        kinv_col = srow.ap()[:, R_Q + 4:R_Q + 5]
        t3 = vec(lambda e: e.tensor_copy(kinv_col, PS(4, 1, 4)), [tb])
        tcs = None
        if cs_col is not None:
            tcs = vec(lambda e: e.memset(ccol(cs_col, nmblk), 0.0),
                      [t3] + R.wdep("cons"), tag=tag + "cs0")
        prevq = [t3]
        for kk in range(nK):
            woff = S0_W0 if kk % 2 == 0 else S0_W1
            wt = FAr(woff, M)
            tl = dma("sync", wt, src[128 * kk:128 * (kk + 1), :],
                     deps=prevq + R.wdep(f"qw{kk % 2}"), tag=tag + "l2")
            R.wr(tl, f"qw{kk % 2}")
            tv = vec(lambda e, w=wt, o=FAr(S0_XN, M): e.tensor_scalar(o, w, kinv_col, None,
                                                     ALU.mult),
                     [tl, t3] + R.wdep("qv"), tag=tag + "v")
            R.rd(tv, f"qw{kk % 2}")
            R.wr(tv, "qv")
            tchunk = tv
            for q0 in range(0, M, 512):
                tr = round_p1(FAr(S0_XN + q0, 512), BAr(q0, 512),
                              [tchunk] + (R.wdep("qba") if q0 == 0 else []), 0.0, 2.0)
                tchunk = vec(lambda e, b=BAr(q0, 512): e.tensor_scalar(
                    b, b, 1.0, None, ALU.subtract), [tr], tag=tag + "sh")
            R.rd(tchunk, "qv")
            R.wr(tchunk, "qba")
            tst = dma("gpsimd", dst[128 * kk:128 * (kk + 1), :], BAr(0, M),
                      deps=[tchunk], tag=tag + "st")
            R.rd(tst, "qba")
            prevq = [tst]
            if cs_col is not None:
                tprev = tcs
                for mm in range(nmblk):
                    tm = mmg(PS(5, 1), [(BAr(mm * 128, 128), ONESB())],
                             deps=[tchunk, tprev] + R.rdep("onesb"), tag=tag + "cs")
                    R.rd(tm, "qba")
                    tprev = vec(lambda e, mm=mm: e.tensor_add(
                        ccol(cs_col + mm), ccol(cs_col + mm), PS(5, 1)), [tm])
                tcs = tprev
                prevq.append(tprev)
        if cs_col is not None:
            R.wr(tcs, "cons2")
        return prevq

    s0_quant(LI[0]["wg"], wq_sd["g"], DM, MI, 0, 1e-5, c_csg, KMI, "qg")
    s0_quant(LI[0]["wu"], wq_sd["u"], DM, MI, 1, 1e-5, c_csu, KMI, "qu")
    s0_quant(LI[0]["wd"], wq_sd["d"], MI, DM, 2, 1e-5, c_csd, KDM, "qd")
    s0_quant(LI[1]["w1"], wq_sd["w1"], DM, MI, 3, 1e-8, None, 0, "q1")
    s0_quant(LI[1]["w2"], wq_sd["w2"], MI, DM, 4, 1e-8, None, 0, "q2")
    sch.barrier()
    use_big()

    def stats_rows(nt, ndim, eps, bsum, bsq, bout1, bout2, deps, mean_sub, tag):
        W0 = R_STAT
        r_m = srow.ap()[0:1, W0:W0 + nt]
        r_v = srow.ap()[0:1, W0 + 512:W0 + 512 + nt]
        r_t = srow.ap()[0:1, W0 + 1024:W0 + 1024 + nt]
        r_r2 = srow.ap()[0:1, W0 + 1536:W0 + 1536 + nt]
        out = []
        if mean_sub:
            t1 = vec(lambda e: e.tensor_scalar(r_m, PS(bsum, nt)[0:1, :],
                                               1.0 / ndim, None, ALU.mult),
                     list(deps), tag=tag + "m")
            t4a = vec(lambda e: e.tensor_scalar(r_v, PS(bsq, nt)[0:1, :],
                                                1.0 / ndim, None, ALU.mult), list(deps))
            t3 = vec(lambda e: e.tensor_mul(r_t, r_m, r_m), [t1])
            t4 = vec(lambda e: e.tensor_sub(r_v, r_v, r_t), [t4a, t3])
        else:
            t4 = vec(lambda e: e.tensor_scalar(r_v, PS(bsq, nt)[0:1, :],
                                               1.0 / ndim, None, ALU.mult), list(deps))
        t5 = vec(lambda e: e.tensor_scalar(r_v, r_v, eps, None, ALU.add), [t4])
        t6 = sca(lambda e: e.activation(out=r_t, in_=r_v, func=ACT.Ln), [t5],
                 tag=tag + "ln")
        t7 = sca(lambda e: e.activation(out=r_t, in_=r_t, func=ACT.Exp, scale=-0.5), [t6])
        t8 = vec(lambda e: e.tensor_mul(r_r2, r_t, r_t), [t7])
        t9 = vec(lambda e: e.tensor_mul(r_r2, r_v, r_r2), [t8])
        t10 = vec(lambda e: e.tensor_scalar(r_r2, r_r2, -0.5, 1.5, ALU.mult, ALU.add), [t9])
        t11 = vec(lambda e: e.tensor_mul(r_t, r_t, r_r2), [t10])
        tb1 = mmg(PS(bout1, nt), [(ONES1, r_t)], deps=[t11] + R.rdep("ones1"),
                  tag=tag + "b1")
        out.append(tb1)
        if mean_sub:
            t12 = vec(lambda e: e.tensor_mul(r_m, r_m, r_t), [t11])
            tb2 = mmg(PS(bout2, nt), [(ONES1, r_m)], deps=[t12] + R.rdep("ones1"),
                      tag=tag + "b2")
            out.append(tb2)
        return out

    def phaseAx(l):
        Lw = LI[l]
        xin = x_ext if l == 0 else x1_s
        for kk in range(KDM):
            tk = dma("sync", FAr(A_WX + kk * 2048, 2048),
                     Lw["w_in_x"][128 * kk:128 * (kk + 1), :],
                     deps=R.wdep(f"wx{kk}"), tag=f"A{l}wx")
            R.wr(tk, f"wx{kk}")
        for nm, col, w_, src in (
            ("convw", c_convw, 64, Lw["conv_w"].rearrange("(b p) j -> p b j", p=128)),
            ("convb", c_convb, 16, Lw["conv_b"].rearrange("(b p) -> p b", p=128)),
            ("dtb", c_dtb, 16, Lw["b_dt"].rearrange("(b p) -> p b", p=128)),
            ("dcol", c_dcol, 16, Lw["d"].rearrange("(b p) -> p b", p=128)),
        ):
            dstap = ccol(col, w_)
            if nm == "convw":
                dstap = dstap.rearrange("p (b j) -> p b j", j=4)
            tk = dma("sync", dstap, src, deps=R.wdep(nm), tag=nm, slow=True)
            R.wr(tk, nm)
        if l == 0:
            tk = dma("sync", ccol(c_dtw, 16),
                     Lw["w_dt"].rearrange("(b p) -> p b", p=128),
                     deps=R.wdep("dtw"), tag="dtw", slow=True)
            R.wr(tk, "dtw")
            tk = dma("sync", ccol(c_nw, 8),
                     Lw["norm_w"].rearrange("(b p) -> p b", p=128),
                     deps=R.wdep("nw"), tag="nw", slow=True)
            R.wr(tk, "nw")
        t1 = dma("sync",
                 srow.ap()[:, R_ALOG:R_ALOG + 128].rearrange("p (b n) -> p b n", n=NS),
                 Lw["a_log"][0:DH, :].rearrange("(b p) n -> p b n", p=128),
                 deps=R.wdep("rrow"), tag="alog", slow=True)
        t2 = sca(lambda e: e.activation(out=srow.ap()[:, R_ALOG + 128:R_ALOG + 256],
                                        in_=srow.ap()[:, R_ALOG:R_ALOG + 128],
                                        func=ACT.Exp), [t1], tag="aexp")
        t3 = vec(lambda e: e.tensor_scalar(ccol(c_ascale, 128),
                                           srow.ap()[:, R_ALOG + 128:R_ALOG + 256],
                                           -1.0, None, ALU.mult),
                 [t2] + R.wdep("ascale"), tag="aneg")
        R.wr(t3, "ascale")
        R.wr(t3, "rrow")
        nxp = 33 if l == 0 else 96
        if l == 1:
            for mm in range(16):
                tk = dma("sync", wdtp.ap()[:, 128 * mm:128 * (mm + 1)],
                         Lw["w_dtp"][:, 128 * mm:128 * (mm + 1)],
                         deps=R.wdep(f"wdtp{mm}"), tag="wdtp")
                R.wr(tk, f"wdtp{mm}")

        def a_tile(it, ntok, halo):
            col0 = 0 if halo else 3 + it * TT
            tl = []
            dd0 = R.wdep("xt")
            for pb in range(KDM):
                tk = dma("sync", FAr(A_XT + pb * TT, ntok),
                         xin[128 * pb:128 * (pb + 1), col0:col0 + ntok],
                         deps=dd0, tag="xld")
                tl.append(tk)
            R.wr(tl[-1], "xt")
            for tk in tl[:-1]:
                R["xt"].r.append(tk)
            def reg3(off, ntok=ntok, fah=arena["fa"]):
                if ntok == TT:
                    return fah.ap()[:, off:off + 8 * TT]
                return fah.ap()[:, off:off + 8 * TT].rearrange(
                    "p (b t) -> p b t", t=TT)[:, :, 0:ntok]
            tsq = sca(lambda e: e.activation(out=reg3(A_SQ), in_=reg3(A_XT),
                                             func=ACT.Square),
                      tl + R.wdep("sq"), tag="sq")
            R.wr(tsq, "sq")
            R.rd(tsq, "xt")
            msum = mmg(PS(4, ntok, 0, 0, 1),
                       [(ONESC, FAr(A_XT + pb * TT, ntok)) for pb in range(KDM)],
                       deps=tl + R.rdep("cons") + R.wdep("ps4"), tag="ssum")
            R.wr(msum, "ps4")
            R.rd(msum, "xt")
            msq = mmg(PS(5, ntok, 0, 0, 1),
                      [(ONESC, FAr(A_SQ + pb * TT, ntok)) for pb in range(KDM)],
                      deps=[tsq] + R.rdep("cons") + R.wdep("ps5"), tag="ssq")
            R.wr(msq, "ps5")
            R.rd(msq, "sq")
            stoks = stats_rows(ntok, DM, EPS_LN, 4, 5, 6, 7,
                               [msum, msq] + R.wdep("ps6", "ps7"), True, "ln")
            R.rd(stoks[0], "ps4", "ps5")
            R.wr(stoks[0], "ps6")
            R.wr(stoks[1], "ps7")
            txn = []
            for pb in range(KDM):
                xnp = FAr(A_XN + pb * TT, ntok)
                ta = vec(lambda e, pb=pb, xnp=xnp: e.tensor_mul(
                    xnp, FAr(A_XT + pb * TT, ntok), PS(6, ntok)),
                    stoks[:1] + tl + R.wdep("xn"), tag="xn1")
                R.rd(ta, "ps6")
                R.rd(ta, "xt")
                tb = vec(lambda e, xnp=xnp: e.tensor_sub(xnp, xnp, PS(7, ntok)),
                         [ta] + stoks[1:], tag="xn2")
                R.rd(tb, "ps7")
                txn.append(tb)
            R.wr(txn[-1], "xn")
            for tb in txn[:-1]:
                R["xn"].r.append(tb)
            if not halo:
                for pb in range(KDM):
                    tk = dma("gpsimd", xn_s[128 * pb:128 * (pb + 1), it * TT:it * TT + ntok],
                             FAr(A_XN + pb * TT, ntok), deps=[txn[pb]], tag="xnst")
                    R.rd(tk, "xn")
            if l == 0:
                tq2 = sca(lambda e: e.activation(out=reg3(A_SQ), in_=reg3(A_XN),
                                                 func=ACT.Square),
                          txn + R.wdep("sq"), tag="sq2")
                R.wr(tq2, "sq")
                R.rd(tq2, "xn")
                msq2 = mmg(PS(5, ntok, 0, 0, 1),
                           [(ONESC, FAr(A_SQ + pb * TT, ntok)) for pb in range(KDM)],
                           deps=[tq2] + R.rdep("cons") + R.wdep("ps5"), tag="ssq2")
                R.wr(msq2, "ps5")
                R.rd(msq2, "sq")
                st2 = stats_rows(ntok, DM, EPS_RMS, 4, 5, 6, 7,
                                 [msq2] + R.wdep("ps6"), False, "rms")
                R.rd(st2[0], "ps5")
                R.wr(st2[0], "ps6")
                txn2 = []
                for pb in range(KDM):
                    tk = vec(lambda e, pb=pb: e.scalar_tensor_tensor(
                        FAr(A_XT + pb * TT, ntok), FAr(A_XN + pb * TT, ntok),
                        ccol(c_nw + pb), PS(6, ntok), ALU.mult, ALU.mult),
                        st2[:1] + [txn[pb]] + R.rdep("nw") + R.wdep("xt"), tag="xn2r")
                    R.rd(tk, "ps6", "xn")
                    txn2.append(tk)
                xn2_off = A_XT
                R.wr(txn2[-1], "xt")
                for tk in txn2[:-1]:
                    R["xt"].r.append(tk)
                tx2 = txn2
                if not halo:
                    for pb in range(KDM):
                        tk = dma("gpsimd",
                                 xn2_s[128 * pb:128 * (pb + 1), it * TT:it * TT + ntok],
                                 FAr(A_XT + pb * TT, ntok), deps=[txn2[pb]], tag="x2st")
                        R.rd(tk, "xt")
            else:
                xn2_off = A_XN
                tx2 = txn
            xp_prev = None
            for pb in range(KDI):
                bank = pb % 4
                grp = mmg(PS(bank, ntok),
                          [(FAr(A_WX + kk * 2048 + pb * 128, 128),
                            FAr(xn2_off + kk * TT, ntok)) for kk in range(KDM)],
                          deps=tx2 + [R[f"wx{kk}"].w for kk in range(KDM)]
                          + R.wdep(f"ps{bank}"), tag="xmm")
                R.wr(grp, f"ps{bank}")
                if l == 0:
                    R.rd(grp, "xt")
                else:
                    R.rd(grp, "xn")
                cslot = A_CIN + (pb % 2) * 520
                if not halo:
                    tt1 = vec(lambda e, pb=pb, cslot=cslot: e.tensor_copy(
                        FAr(cslot, 3), FAr(A_TK + pb * 3, 3)),
                        R.wdep(f"cin{pb % 2}") + R.rdep(f"tk{pb}"), tag="tin")
                    R.rd(tt1, f"tk{pb}")
                else:
                    tt1 = None
                tt2 = vec(lambda e, cslot=cslot, bank=bank, ntok=ntok: e.tensor_copy(
                    FAr(cslot + 3, ntok), PS(bank, ntok)),
                    [grp, tt1] + (R.wdep(f"cin{pb % 2}") if halo else []), tag="evx")
                R.rd(tt2, f"ps{bank}")
                R.wr(tt2, f"cin{pb % 2}")
                if halo:
                    tt3 = vec(lambda e, pb=pb, cslot=cslot: e.tensor_copy(
                        FAr(A_TK + pb * 3, 3), FAr(cslot + 3, 3)),
                        [tt2] + R.wdep(f"tk{pb}"), tag="thalo")
                    R.wr(tt3, f"tk{pb}")
                    R.rd(tt3, f"cin{pb % 2}")
                    continue
                uslot = A_U + (pb % 2) * TT
                ca = vec(lambda e, pb=pb, cslot=cslot, uslot=uslot: e.tensor_scalar(
                    FAr(uslot, ntok), FAr(cslot + 0, ntok),
                    ccol(c_convw + pb * 4), ccol(c_convb + pb),
                    ALU.mult, ALU.add),
                    [tt2] + R.rdep("convw", "convb") + R.wdep(f"u{pb % 2}"), tag="cv0")
                R.rd(ca, f"cin{pb % 2}")
                for j in (1, 2, 3):
                    ca = vec(lambda e, pb=pb, j=j, cslot=cslot, uslot=uslot:
                             e.scalar_tensor_tensor(
                                 FAr(uslot, ntok), FAr(cslot + j, ntok),
                                 ccol(c_convw + pb * 4 + j),
                                 FAr(uslot, ntok), ALU.mult, ALU.add), [ca], tag="cvj")
                    R.rd(ca, f"cin{pb % 2}")
                tt4 = vec(lambda e, pb=pb, cslot=cslot, ntok=ntok: e.tensor_copy(
                    FAr(A_TK + pb * 3, 3), FAr(cslot + ntok, 3)),
                    [tt2] + R.wdep(f"tk{pb}"), tag="tout")
                R.wr(tt4, f"tk{pb}")
                R.rd(tt4, f"cin{pb % 2}")
                cu = sca(lambda e, uslot=uslot: e.activation(
                    out=FAr(uslot, ntok), in_=FAr(uslot, ntok), func=ACT.Silu),
                    [ca], tag="silu")
                R.wr(cu, f"u{pb % 2}")
                tk = dma("gpsimd", u_s[128 * pb:128 * (pb + 1), it * TT:it * TT + ntok],
                         FAr(uslot, ntok), deps=[cu], tag="ust")
                R.rd(tk, f"u{pb % 2}")
                wxt = FAr(A_ZE + (pb % 2) * 512, nxp)
                tk2 = dma("sync", wxt, Lw["w_xproj"][128 * pb:128 * (pb + 1), :],
                          deps=R.wdep(f"ze{pb % 2}"), tag="wxp")
                R.wr(tk2, f"ze{pb % 2}")
                mm1 = ten(lambda e, pb=pb, wxt=wxt, ntok=ntok, nxp=nxp: e.matmul(
                    PS(6, ntok, 0, 0, nxp), wxt, FAr(A_U + (pb % 2) * TT, ntok),
                    start=(pb == 0), stop=(pb == KDI - 1)),
                    [cu, tk2] + ([xp_prev] if xp_prev else R.wdep("ps6")), tag="xpmm")
                R.rd(mm1, f"u{pb % 2}", f"ze{pb % 2}")
                xp_prev = mm1
            if halo:
                return
            R.wr(xp_prev, "ps6")
            te = vec(lambda e, ntok=ntok, nxp=nxp: e.tensor_copy(
                FAr(A_XD, ntok, 0, nxp), PS(6, ntok, 0, 0, nxp)),
                [xp_prev] + R.wdep("xd"), tag="xdev")
            R.rd(te, "ps6")
            R.wr(te, "xd")
            brow = 1 if l == 0 else 64
            tk = dma("gpsimd", cc[f"bc_in{l}"][0:32, it * TT:it * TT + ntok],
                     FAr(A_XD, ntok, brow, brow + 32), deps=[te], tag="bcst")
            R.rd(tk, "xd")
            if l == 0:
                tbd = mmg(PS(4, ntok), [(ONES1, FAr(A_XD, ntok, 0, 1))],
                          deps=[te] + R.rdep("ones1") + R.wdep("ps4"), tag="dtb")
                R.wr(tbd, "ps4")
                R.rd(tbd, "xd")
            for pb in range(KDI):
                pre = FAr(A_DT, ntok)
                t2a = FAr(A_DT + 512, ntok)
                if l == 0:
                    tp = vec(lambda e, pb=pb: e.tensor_scalar(
                        pre, PS(4, ntok), ccol(c_dtw + pb), ccol(c_dtb + pb),
                        ALU.mult, ALU.add),
                        R.rdep("ps4") + R.rdep("dtw", "dtb") + R.wdep("dt0"), tag="pre")
                    R.rd(tp, "ps4")
                else:
                    bank = pb % 4
                    mmq = ten(lambda e, pb=pb, bank=bank, ntok=ntok: e.matmul(
                        PS(bank, ntok), wdtp.ap()[:, 128 * pb:128 * (pb + 1)],
                        FAr(A_XD, ntok, 0, 64), start=True, stop=True),
                        [te, R[f"wdtp{pb}"].w] + R.wdep(f"ps{bank}"), tag="dtmm")
                    R.rd(mmq, "xd")
                    R.wr(mmq, f"ps{bank}")
                    tp = vec(lambda e, pb=pb, bank=bank: e.tensor_scalar(
                        pre, PS(bank, ntok), ccol(c_dtb + pb), None, ALU.add),
                        [mmq] + R.rdep("dtb") + R.wdep("dt0"), tag="pre1")
                    R.rd(tp, f"ps{bank}")
                R.wr(tp, "dt0")
                ta = sca(lambda e: e.activation(out=t2a, in_=pre, func=ACT.Abs),
                         [tp] + R.wdep("dt1"), tag="dabs")
                tb2 = sca(lambda e: e.activation(out=t2a, in_=t2a, func=ACT.Exp,
                                                 scale=-1.0), [ta])
                tc2 = sca(lambda e: e.activation(out=t2a, in_=t2a, func=ACT.Ln,
                                                 bias=ONESC), [tb2])
                R.rd(tc2, "cons")
                R.wr(tc2, "dt1")
                td = vec(lambda e: e.scalar_tensor_tensor(
                    pre, pre, 0.0, t2a, ALU.max, ALU.add), [tc2, tp], tag="delta")
                R.wr(td, "dt0")
                R.rd(td, "dt1")
                if pb < NDB:
                    tk = dma("gpsimd", m_loc[128 * pb:128 * (pb + 1), it * TT:it * TT + ntok],
                             pre, deps=[td], tag="mst")
                else:
                    tk = dma("gpsimd",
                             cc[f"m_in{l}"][128 * (pb - NDB):128 * (pb - NDB + 1),
                                            it * TT:it * TT + ntok],
                             pre, deps=[td], tag="mstc")
                R.rd(tk, "dt0")
                ure = FAr(A_DT + 1024, ntok)
                tk2 = dma("sync", ure, u_s[128 * pb:128 * (pb + 1), it * TT:it * TT + ntok],
                          deps=[td] + R.wdep("dt2"), tag="url")
                R.wr(tk2, "dt2")
                tw = vec(lambda e, o=FAr(A_WW, ntok): e.tensor_mul(o, pre, ure),
                         [td, tk2] + R.wdep("ww"), tag="wmul")
                R.rd(tw, "dt0", "dt2")
                R.wr(tw, "ww")
                if pb < NDB:
                    tk3 = dma("gpsimd", w_loc[128 * pb:128 * (pb + 1), it * TT:it * TT + ntok],
                              FAr(A_WW, ntok), deps=[tw], tag="wst")
                else:
                    tk3 = dma("gpsimd",
                              cc[f"w_in{l}"][128 * (pb - NDB):128 * (pb - NDB + 1),
                                             it * TT:it * TT + ntok],
                              FAr(A_WW, ntok), deps=[tw], tag="wstc")
                R.rd(tk3, "ww")

        a_tile(0, 3, True)
        for it in range(NTILE):
            a_tile(it, TT, False)
        sch.barrier()

    def phaseAz(l):
        Lw = LI[l]
        src2 = xn2_s if l == 0 else xn_s
        for kk in range(KDM):
            tk = dma("sync", FAr(A_WX + kk * 2048, 2048),
                     Lw["w_in_z"][128 * kk:128 * (kk + 1), :],
                     deps=R.wdep(f"wx{kk}"), tag=f"A{l}wz")
            R.wr(tk, f"wx{kk}")
        for it in range(NTILE):
            tl = []
            dd0 = R.wdep("xn")
            for kk in range(KDM):
                tk = dma("sync", FAr(A_XN + kk * TT, TT),
                         src2[128 * kk:128 * (kk + 1), it * TT:(it + 1) * TT],
                         deps=dd0, tag="zxl")
                tl.append(tk)
            R.wr(tl[-1], "xn")
            for tk in tl[:-1]:
                R["xn"].r.append(tk)
            for pb in range(KDI):
                bank = pb % 4
                grp = mmg(PS(bank, TT),
                          [(FAr(A_WX + kk * 2048 + pb * 128, 128),
                            FAr(A_XN + kk * TT, TT)) for kk in range(KDM)],
                          deps=tl + [R[f"wx{kk}"].w for kk in range(KDM)]
                          + R.wdep(f"ps{bank}"), tag="zmm")
                R.wr(grp, f"ps{bank}")
                R.rd(grp, "xn")
                ze = FAr(A_ZE + (pb % 2) * 512, TT)
                tv2 = vec(lambda e, bank=bank, ze=ze: e.tensor_copy(ze, PS(bank, TT)),
                          [grp] + R.wdep(f"ze{pb % 2}"), tag="zev")
                R.rd(tv2, f"ps{bank}")
                R.wr(tv2, f"ze{pb % 2}")
                tk = dma("gpsimd", z_s[128 * pb:128 * (pb + 1), it * TT:(it + 1) * TT],
                         ze, deps=[tv2], tag="zst")
                R.rd(tk, f"ze{pb % 2}")
        sch.barrier()

    def row_ops_scale(row_off, kcell_col, krow, deps, tag):
        rmax = srow.ap()[0:1, row_off:row_off + 512]
        rs = srow.ap()[0:1, row_off + 512:row_off + 1024]
        rb = srow.ap()[0:1, row_off + 1024:row_off + 1536]
        t1 = vec(lambda e: e.tensor_scalar(rmax, rmax, 1e-5, None, ALU.max),
                 list(deps) + R.wdep("brows"), tag=tag + "cl")
        t2 = vec(lambda e: e.reciprocal(rs, rmax), [t1])
        t3 = vec(lambda e: e.tensor_scalar(rs, rs, 127.0, None, ALU.mult), [t2])
        t4 = vec(lambda e: e.tensor_scalar(rb, rmax, 1.0 / 127.0, None, ALU.mult), [t1])
        t5 = vec(lambda e: e.tensor_scalar(
            rb, rb, cons.ap()[0:1, c_kap + krow:c_kap + krow + 1], None,
            ALU.mult), [t4], tag=tag + "kb")
        R.wr(t5, "brows")
        return t3, t5

    def phaseB0():
        for kk in range(KDM):
            tk = dma("sync", WQv(kk * MI, MI), wq_sd["g"][128 * kk:128 * (kk + 1), :],
                     deps=R.wdep(f"bw{kk}"), tag="Bwg")
            R.wr(tk, f"bw{kk}")
        for it in range(NTILE):
            tl = []
            dd0 = R.wdep("bxn")
            for pb in range(KDM):
                tk = dma("sync", FAr(B_XN + pb * TT, TT),
                         xn_s[128 * pb:128 * (pb + 1), it * TT:(it + 1) * TT],
                         deps=dd0, tag="bxl")
                tl.append(tk)
            R.wr(tl[-1], "bxn")
            for tk in tl[:-1]:
                R["bxn"].r.append(tk)
            acc = FAr(B_T0, TT)
            tab = sca(lambda e, o=FAr(B_ABS, 8 * TT): e.activation(out=o,
                                             in_=FAr(B_XN, 8 * TT), func=ACT.Abs),
                      tl + R.wdep("babs"), tag="babs")
            R.rd(tab, "bxn")
            R.wr(tab, "babs")
            tprev = vec(lambda e, a=FAr(B_ABS, TT), b=FAr(B_ABS + TT, TT): e.tensor_max(acc, a, b),
                        [tab] + R.wdep("bt0"), tag="amx")
            for pb in range(2, KDM):
                tprev = vec(lambda e, pb=pb: e.tensor_max(
                    acc, acc, FAr(B_ABS + pb * TT, TT)), [tprev], tag="amx")
            R.rd(tprev, "babs")
            R.wr(tprev, "bt0")
            tpm = pmax_row(acc, R_BR, [tprev], tag="pmx")
            R.rd(tpm, "bt0")
            ts, tb = row_ops_scale(R_BR, c_kap, 0, [tpm], "sg")
            tkst = dma("gpsimd", srow_s[0:1, it * TT:(it + 1) * TT],
                       srow.ap()[0:1, R_BR:R_BR + 512], deps=[ts, tb],
                       tag="sst")
            R.rd(tkst, "brows")
            tb6 = mmg(PS(6, TT), [(ONES1, srow.ap()[0:1, R_BR + 512:R_BR + 1024])],
                      deps=[ts] + R.rdep("ones1") + R.wdep("ps6"), tag="sbc")
            R.wr(tb6, "ps6")
            R.rd(tb6, "brows")
            tb5m = mmg(PS(5, TT), [(ONES1, srow.ap()[0:1, R_BR + 1024:R_BR + 1536])],
                      deps=[tb] + R.rdep("ones1") + R.wdep("ps5"), tag="gbc")
            R.wr(tb5m, "ps5")
            R.rd(tb5m, "brows")
            tb5 = vec(lambda e, o=FAr(B_SB5, TT): e.tensor_copy(o, PS(5, TT)),
                      [tb5m] + R.wdep("bsb5"), tag="gbce")
            R.rd(tb5, "ps5")
            R.wr(tb5, "bsb5")
            tq = []
            for pb in range(KDM):
                v = FAr(B_T0 + 512, TT)
                tv = vec(lambda e, pb=pb, v=v: e.tensor_mul(
                    v, FAr(B_XN + pb * TT, TT), PS(6, TT)),
                    [tb6, tl[pb]] + R.wdep("bt1"), tag="qv")
                R.rd(tv, "ps6", "bxn")
                R.wr(tv, "bt1")
                tr = round_p1(v, BAr(pb * TT, TT),
                              [tv] + (R.wdep("bq") if pb == 0 else []), -127.0, 128.0)
                R.rd(tr, "bt1")
                tq.append(tr)
                tk = dma("gpsimd", xq_s[128 * pb:128 * (pb + 1), it * TT:(it + 1) * TT],
                         BAr(pb * TT, TT), deps=[tr], tag="xqst")
                tq.append(tk)
            R.wr(tq[-2], "bq")
            for tk in tq:
                R.rd(tk, "bq") if tk[2] else None
            R.rd(tq[-1], "bq")
            for mm in range(KMI):
                bank = mm % 4
                grp = mmg(PS(bank, TT),
                          [(WQv(kk * MI + mm * 128, 128), BAr(kk * TT, TT))
                           for kk in range(KDM)],
                          deps=tq + [R[f"bw{kk}"].w for kk in range(KDM)]
                          + R.wdep(f"ps{bank}"), tag="gmm")
                R.wr(grp, f"ps{bank}")
                R.rd(grp, "bq")
                go = FAr(B_T0 + 1024, TT)
                tg = vec(lambda e, bank=bank, mm=mm, go=go: e.scalar_tensor_tensor(
                    go, PS(bank, TT), ccol(c_csg + mm), FAr(B_SB5, TT),
                    ALU.subtract, ALU.mult),
                    [grp, tb5] + R.rdep("cons2") + R.wdep("bt2"), tag="gep")
                R.rd(tg, f"ps{bank}", "bsb5")
                R.wr(tg, "bt2")
                tsg = sca(lambda e, go=go: e.activation(out=go, in_=go, func=ACT.Silu),
                          [tg], tag="gsl")
                R.wr(tsg, "bt2")
                tk = dma("gpsimd", sg_s[128 * mm:128 * (mm + 1), it * TT:(it + 1) * TT],
                         go, deps=[tsg], tag="sgst")
                R.rd(tk, "bt2")
        sch.barrier()
        for kk in range(KDM):
            tk = dma("sync", WQv(kk * MI, MI), wq_sd["u"][128 * kk:128 * (kk + 1), :],
                     deps=R.wdep(f"bw{kk}"), tag="Bwu")
            R.wr(tk, f"bw{kk}")
        for it in range(NTILE):
            tq = []
            dd0 = R.wdep("bq")
            for pb in range(KDM):
                tk = dma("sync", BAr(pb * TT, TT),
                         xq_s[128 * pb:128 * (pb + 1), it * TT:(it + 1) * TT],
                         deps=dd0, tag="xql")
                tq.append(tk)
            R.wr(tq[-1], "bq")
            tms = dma("sync", srow.ap()[0:1, R_BR:R_BR + 512],
                      srow_s[0:1, it * TT:(it + 1) * TT],
                      deps=R.wdep("pm") + R.wdep("brows"), tag="sld")
            R.wr(tms, "pm")
            R.wr(tms, "brows")
            ts, tb = row_ops_scale(R_BR, c_kap, 1, [tms], "su")
            tb5m = mmg(PS(5, TT),
                      [(ONES1, srow.ap()[0:1, R_BR + 1024:R_BR + 1536])],
                      deps=[tb] + R.rdep("ones1") + R.wdep("ps5"), tag="ubc")
            R.wr(tb5m, "ps5")
            R.rd(tb5m, "brows")
            tb5 = vec(lambda e, o=FAr(B_SB5, TT): e.tensor_copy(o, PS(5, TT)),
                      [tb5m] + R.wdep("bsb5"), tag="ubce")
            R.rd(tb5, "ps5")
            R.wr(tb5, "bsb5")
            gacc = FAr(B_T0 + 2048, TT)
            tga = vec(lambda e: e.memset(gacc, 0.0), R.wdep("bt3"), tag="gz")
            R.wr(tga, "bt3")
            for mm in range(KMI):
                bank = mm % 4
                grp = mmg(PS(bank, TT),
                          [(WQv(kk * MI + mm * 128, 128), BAr(kk * TT, TT))
                           for kk in range(KDM)],
                          deps=tq + [R[f"bw{kk}"].w for kk in range(KDM)]
                          + R.wdep(f"ps{bank}"), tag="umm")
                R.wr(grp, f"ps{bank}")
                R.rd(grp, "bq")
                uo = FAr(B_T0, TT)
                tu = vec(lambda e, bank=bank, mm=mm, uo=uo: e.scalar_tensor_tensor(
                    uo, PS(bank, TT), ccol(c_csu + mm), FAr(B_SB5, TT),
                    ALU.subtract, ALU.mult),
                    [grp, tb5] + R.rdep("cons2") + R.wdep("bt0"), tag="uep")
                R.rd(tu, f"ps{bank}", "bsb5")
                R.wr(tu, "bt0")
                sgt = FAr(B_T0 + 512, TT)
                tk = dma("sync", sgt, sg_s[128 * mm:128 * (mm + 1), it * TT:(it + 1) * TT],
                         deps=R.wdep("bt1"), tag="sgl")
                R.wr(tk, "bt1")
                tgu = vec(lambda e, uo=uo, sgt=sgt: e.tensor_mul(uo, uo, sgt),
                          [tu, tk], tag="gumul")
                R.rd(tgu, "bt1")
                R.wr(tgu, "bt0")
                tk2 = dma("gpsimd", gu_s[128 * mm:128 * (mm + 1), it * TT:(it + 1) * TT],
                          uo, deps=[tgu], tag="gust")
                R.rd(tk2, "bt0")
                tab2 = sca(lambda e, uo=uo: e.activation(
                    out=FAr(B_ABS, TT), in_=uo, func=ACT.Abs),
                    [tgu] + R.wdep("babs"), tag="gabs")
                R.rd(tab2, "bt0")
                R.wr(tab2, "babs")
                tga = vec(lambda e, b=FAr(B_ABS, TT): e.tensor_max(gacc, gacc, b),
                          [tab2, tga], tag="gumx")
                R.rd(tga, "babs")
            R.wr(tga, "bt3")
            tpm = pmax_row(gacc, R_BR, [tga], tag="pmg")
            R.rd(tpm, "bt3")
            ts2, tb2 = row_ops_scale(R_BR, c_kap, 2, [tpm], "sd")
            tk = dma("gpsimd", srow_s[1:2, it * TT:(it + 1) * TT],
                     srow.ap()[0:1, R_BR + 512:R_BR + 1024],
                     deps=[ts2], tag="sg2st")
            R.rd(tk, "brows")
            tk = dma("gpsimd", srow_s[2:3, it * TT:(it + 1) * TT],
                     srow.ap()[0:1, R_BR + 1024:R_BR + 1536],
                     deps=[tb2], tag="sg3st")
            R.rd(tk, "pm")
            R.rd(tk, "brows")
        sch.barrier()
        for kk in range(KMI):
            tk = dma("sync", WQv(kk * 512, 512),
                     wq_sd["d"][128 * kk:128 * (kk + 1), 0:512],
                     deps=R.wdep(f"bw{kk}"), tag="Bwd")
            R.wr(tk, f"bw{kk}")
            tk = dma("sync", WQv(kk * 512 + 16384, 512),
                     wq_sd["d"][128 * kk:128 * (kk + 1), 512:1024],
                     deps=R.wdep(f"bw2{kk}"), tag="Bwd2")
            R.wr(tk, f"bw2{kk}")
        for it in range(NTILE):
            tms = dma("sync", srow.ap()[0:1, R_BR:R_BR + 512],
                      srow_s[1:2, it * TT:(it + 1) * TT],
                      deps=R.wdep("pm") + R.wdep("brows"), tag="sld2")
            R.wr(tms, "pm")
            R.wr(tms, "brows")
            tmb = dma("sync", srow.ap()[0:1, R_BR + 1024:R_BR + 1536],
                      srow_s[2:3, it * TT:(it + 1) * TT], deps=[tms], tag="sld3")
            R.wr(tmb, "brows2")
            tb6 = mmg(PS(6, TT), [(ONES1, srow.ap()[0:1, R_BR:R_BR + 512])],
                      deps=[tms] + R.rdep("ones1") + R.wdep("ps6"), tag="dsbc")
            R.wr(tb6, "ps6")
            R.rd(tb6, "brows")
            tb5m = mmg(PS(5, TT), [(ONES1, srow.ap()[0:1, R_BR + 1024:R_BR + 1536])],
                      deps=[tmb] + R.rdep("ones1") + R.wdep("ps5"), tag="dbbc")
            R.wr(tb5m, "ps5")
            R.rd(tb5m, "brows2")
            tb5 = vec(lambda e, o=FAr(B_SB5, TT): e.tensor_copy(o, PS(5, TT)),
                      [tb5m] + R.wdep("bsb5"), tag="dbbe")
            R.rd(tb5, "ps5")
            R.wr(tb5, "bsb5")
            tqs = []
            for kk in range(KMI):
                gt = FAr(B_T0 + (kk % 2) * 512, TT)
                tk = dma("sync", gt, gu_s[128 * kk:128 * (kk + 1), it * TT:(it + 1) * TT],
                         deps=R.wdep(f"bt{kk % 2}"), tag="gul")
                R.wr(tk, f"bt{kk % 2}")
                tv = vec(lambda e, gt=gt: e.tensor_mul(gt, gt, PS(6, TT)), [tk, tb6],
                         tag="gqv")
                R.rd(tv, "ps6")
                R.wr(tv, f"bt{kk % 2}")
                tr = round_p1(gt, BAr(kk * TT, TT),
                              [tv] + (R.wdep("bq") if kk == 0 else []), -127.0, 128.0)
                R.rd(tr, f"bt{kk % 2}")
                tqs.append(tr)
            R.wr(tqs[-1], "bq")
            for mm in range(KDM):
                bank = mm % 4
                pairs = [(WQv(kk * 512 + (0 if mm < 4 else 16384) + (mm % 4) * 128, 128),
                          BAr(kk * TT, TT)) for kk in range(KMI)]
                grp = mmg(PS(bank, TT), pairs,
                          deps=tqs + [R[f"bw{kk}"].w for kk in range(KMI)]
                          + [R[f"bw2{kk}"].w for kk in range(KMI)]
                          + R.wdep(f"ps{bank}"), tag="dmm")
                R.wr(grp, f"ps{bank}")
                R.rd(grp, "bq")
                go = FAr(B_T0 + 1024, TT)
                tg = vec(lambda e, bank=bank, mm=mm, go=go: e.scalar_tensor_tensor(
                    go, PS(bank, TT), ccol(c_csd + mm), FAr(B_SB5, TT),
                    ALU.subtract, ALU.mult),
                    [grp, tb5] + R.rdep("cons2") + R.wdep("bt2"), tag="dep")
                R.rd(tg, f"ps{bank}", "bsb5")
                R.wr(tg, "bt2")
                tsg = sca(lambda e, go=go: e.activation(out=go, in_=go, func=ACT.Silu),
                          [tg], tag="dsl")
                R.wr(tsg, "bt2")
                tk = dma("gpsimd", g_s[128 * mm:128 * (mm + 1), it * TT:(it + 1) * TT],
                         go, deps=[tsg], tag="gst")
                R.rd(tk, "bt2")
        sch.barrier()

    def phaseB1():
        tkb = dma("sync", ccol(c_b1, 32), LI[1]["b1"].rearrange("(b p) -> p b", p=128),
                  deps=R.wdep("b1"), tag="b1l", slow=True)
        R.wr(tkb, "b1")
        tkb2 = dma("sync", ccol(c_b2, 8), LI[1]["b2"].rearrange("(b p) -> p b", p=128),
                   deps=R.wdep("b2"), tag="b2l", slow=True)
        R.wr(tkb2, "b2")
        for kk in range(KDM):
            tk = dma("sync", WQv(kk * MI, MI), wq_sd["w1"][128 * kk:128 * (kk + 1), :],
                     deps=R.wdep(f"bw{kk}"), tag="Bw1")
            R.wr(tk, f"bw{kk}")
        for it in range(NTILE):
            tl = []
            dd0 = R.wdep("bxn")
            for pb in range(KDM):
                tk = dma("sync", FAr(B_XN + pb * TT, TT),
                         xn_s[128 * pb:128 * (pb + 1), it * TT:(it + 1) * TT],
                         deps=dd0, tag="bxl1")
                tl.append(tk)
            R.wr(tl[-1], "bxn")
            for tk in tl[:-1]:
                R["bxn"].r.append(tk)
            tq = []
            for pb in range(KDM):
                th = sca(lambda e, pb=pb: e.activation(
                    out=BAr(pb * TT, TT), in_=FAr(B_XN + pb * TT, TT), func=ACT.Copy),
                    [tl[pb]] + R.wdep("bq"), tag="hic")
                tlo = vec(lambda e, pb=pb: e.tensor_sub(
                    BAr(4096 + pb * TT, TT), FAr(B_XN + pb * TT, TT), BAr(pb * TT, TT)),
                    [th], tag="loc")
                R.rd(tlo, "bxn")
                tq.append(tlo)
            R.wr(tq[-1], "bq")
            for mm in range(KMI):
                bank = mm % 4
                pairs = ([(WQv(kk * MI + mm * 128, 128), BAr(kk * TT, TT))
                          for kk in range(KDM)]
                         + [(WQv(kk * MI + mm * 128, 128), BAr(4096 + kk * TT, TT))
                            for kk in range(KDM)])
                grp = mmg(PS(bank, TT), pairs,
                          deps=tq + [R[f"bw{kk}"].w for kk in range(KDM)]
                          + R.wdep(f"ps{bank}"), tag="w1mm")
                R.wr(grp, f"ps{bank}")
                R.rd(grp, "bq")
                h1 = FAr(B_T0, TT)
                tg = sca(lambda e, bank=bank, mm=mm, h1=h1: e.activation(
                    out=h1, in_=PS(bank, TT), func=ACT.Gelu, bias=ccol(c_b1 + mm)),
                    [grp] + R.rdep("b1") + R.wdep("bt0"), tag="gelu")
                R.rd(tg, f"ps{bank}")
                R.wr(tg, "bt0")
                th2 = sca(lambda e, h1=h1: e.activation(
                    out=BAr(8192, TT), in_=h1, func=ACT.Copy),
                    [tg] + R.wdep("bhl"), tag="h1h")
                tl2 = vec(lambda e, h1=h1: e.tensor_sub(BAr(8192 + TT, TT), h1,
                                                        BAr(8192, TT)), [th2], tag="h1l")
                R.rd(tl2, "bt0")
                R.wr(tl2, "bhl")
                tk = dma("gpsimd", h1hi_s[128 * mm:128 * (mm + 1), it * TT:(it + 1) * TT],
                         BAr(8192, TT), deps=[tl2], tag="h1hst")
                tk2 = dma("gpsimd", h1lo_s[128 * mm:128 * (mm + 1), it * TT:(it + 1) * TT],
                          BAr(8192 + TT, TT), deps=[tl2], tag="h1lst")
                R.rd(tk, "bhl")
                R.rd(tk2, "bhl")
        sch.barrier()
        for kk in range(KMI):
            tk = dma("sync", WQv(kk * 512, 512),
                     wq_sd["w2"][128 * kk:128 * (kk + 1), 0:512],
                     deps=R.wdep(f"bw{kk}"), tag="Bw2a")
            R.wr(tk, f"bw{kk}")
            tk = dma("sync", WQv(kk * 512 + 16384, 512),
                     wq_sd["w2"][128 * kk:128 * (kk + 1), 512:1024],
                     deps=R.wdep(f"bw2{kk}"), tag="Bw2b")
            R.wr(tk, f"bw2{kk}")
        for it in range(NTILE):
            open_grp = {}
            for half in range(2):
                tq = []
                dd0 = R.wdep("bq")
                for kx in range(16):
                    kk = half * 16 + kx
                    t1 = dma("sync", BAr(kx * TT, TT),
                             h1hi_s[128 * kk:128 * (kk + 1), it * TT:(it + 1) * TT],
                             deps=dd0, tag="h1hl")
                    t2 = dma("sync", BAr(8192 + kx * TT, TT),
                             h1lo_s[128 * kk:128 * (kk + 1), it * TT:(it + 1) * TT],
                             deps=dd0, tag="h1ll")
                    tq += [t1, t2]
                R.wr(tq[-1], "bq")
                for tk in tq[:-1]:
                    R["bq"].r.append(tk)
                for mm in range(KDM):
                    bank = mm
                    pairs = []
                    for kx in range(16):
                        kk = half * 16 + kx
                        wv = WQv(kk * 512 + (0 if mm < 4 else 16384) + (mm % 4) * 128, 128)
                        pairs.append((wv, BAr(kx * TT, TT)))
                        pairs.append((wv, BAr(8192 + kx * TT, TT)))

                    def emit(eng, pairs=tuple(pairs), bank=bank, half=half):
                        ins = None
                        n = len(pairs)
                        for i, (l2, r2) in enumerate(pairs):
                            ins = eng.matmul(PS(bank, TT), l2, r2,
                                             start=(half == 0 and i == 0),
                                             stop=(half == 1 and i == n - 1))
                        return ins
                    deps = tq + [R[f"bw{kk}"].w for kk in range(KMI)] \
                        + [R[f"bw2{kk}"].w for kk in range(KMI)]
                    if half == 0:
                        deps = deps + R.wdep(f"ps{bank}")
                    else:
                        deps = deps + [open_grp[bank]]
                    g2 = sch.op("tensor", emit, deps, tag="w2mm")
                    open_grp[bank] = g2
                    if half == 1:
                        R.wr(g2, f"ps{bank}")
                        R.rd(g2, "bq")
                    else:
                        R.rd(g2, "bq")
            for mm in range(KDM):
                go = FAr(B_T0 + (mm % 2) * 512, TT)
                tg = sca(lambda e, mm=mm, go=go: e.activation(
                    out=go, in_=PS(mm, TT), func=ACT.Silu, bias=ccol(c_b2 + mm)),
                    [open_grp[mm]] + R.rdep("b2") + R.wdep(f"bt{mm % 2}"), tag="w2ep")
                R.rd(tg, f"ps{mm}")
                R.wr(tg, f"bt{mm % 2}")
                tk = dma("gpsimd", g_s[128 * mm:128 * (mm + 1), it * TT:(it + 1) * TT],
                         go, deps=[tg], tag="gst1")
                R.rd(tk, f"bt{mm % 2}")
        sch.barrier()

    def phaseC(l):
        tz = vec(lambda e: e.memset(carry.ap()[:, :], 0.0), R.wdep("carry"), tag="cz")
        R.wr(tz, "carry")
        tsel = dma("sync", FAr(C_SEL, 32 * 128, 0, 32), seln_in[:, :],
                   deps=R.wdep("csel"), tag="sel")
        R.wr(tsel, "csel")
        agy = []
        for c in range(NCHUNK):
            lc = (c % 4) * TC
            cb0 = FAr(C_A, TC, 0, 32)
            cb1 = FAr(C_A + TC, TC, 0, 32)
            dst_bc = FAr(C_BC, TC, 0, 32)
            ta = dma("sync", dst_bc, cc[f"bc_in{l}"][0:32, lc:lc + TC],
                     deps=R.wdep("cbc"), tag="bcl0")
            tb = dma("sync", cb0, cc[f"bc_out{l}"][0:32, lc:lc + TC],
                     deps=R.wdep("cA0") + [R[f"ag_bc{l}"].w], tag="bcl1")
            tc3 = dma("sync", cb1, cc[f"bc_out{l}"][32:64, lc:lc + TC],
                      deps=R.wdep("cA1") + [tb], tag="bcl2")
            R.wr(tb, "cA0")
            R.wr(tc3, "cA1")
            t1 = vec(lambda e, m=mcol(3 * c)[0:32, :], dst=dst_bc: e.tensor_scalar(
                dst, dst, m, None, ALU.mult), [ta] + R.rdep("msk"), tag="bcc1")
            t2 = vec(lambda e, m=mcol(3 * c + 1)[0:32, :], dst=dst_bc, s=cb0:
                     e.scalar_tensor_tensor(dst, s, m, dst, ALU.mult, ALU.add),
                     [t1, tb], tag="bcc2")
            t3 = vec(lambda e, m=mcol(3 * c + 2)[0:32, :], dst=dst_bc, s=cb1:
                     e.scalar_tensor_tensor(dst, s, m, dst, ALU.mult, ALU.add),
                     [t2, tc3], tag="bcc3")
            R.wr(t3, "cbc")
            R.rd(t2, "cA0")
            R.rd(t3, "cA1")
            tmw = {}
            for j, (loc, s0t) in enumerate(((m_loc, "m"), (w_loc, "w"))):
                o0 = cc[f"{s0t}_out{l}"]
                for g in range(2):
                    dst = FAr(C_MW + j * 8192 + g * 4096, 4096)
                    dl0, dl1, dl2 = [], [], []
                    for q in range(C_G):
                        db = g * C_G + q
                        dl0.append(dma("sync",
                                       FAr(C_MW + j * 8192 + db * TC, TC),
                                       loc[128 * db:128 * (db + 1), lc:lc + TC],
                                       deps=R.wdep(f"mw{j}{g}") if q == 0 else [],
                                       tag="mwl0"))
                        dl1.append(dma("sync", FAr(C_A + q * TC, TC),
                                       o0[128 * db:128 * (db + 1), lc:lc + TC],
                                       deps=R.wdep(f"cA{q}")
                                       + [R[f"ag_{s0t}{l}"].w], tag="mwl1"))
                        dl2.append(dma("sync", FAr(C_B + q * TC, TC),
                                       o0[DH + 128 * db:DH + 128 * (db + 1),
                                          lc:lc + TC],
                                       deps=(R.wdep("cB") if q == 0 else [])
                                       + [R[f"ag_{s0t}{l}"].w], tag="mwl2"))
                    for q in range(C_G):
                        R.wr(dl1[q], f"cA{q}")
                    R.wr(dl2[-1], "cB")
                    u1 = vec(lambda e, dst=dst, m=mcol(3 * c): e.tensor_scalar(
                        dst, dst, m, None, ALU.mult), dl0 + R.rdep("msk"),
                        tag="mwc1")
                    u2 = vec(lambda e, dst=dst, m=mcol(3 * c + 1),
                             s=FAr(C_A, C_G * TC): e.scalar_tensor_tensor(
                        dst, s, m, dst, ALU.mult, ALU.add), [u1] + dl1,
                        tag="mwc2")
                    for q in range(C_G):
                        R.rd(u2, f"cA{q}")
                    u3 = vec(lambda e, dst=dst, m=mcol(3 * c + 2),
                             s=FAr(C_B, C_G * TC): e.scalar_tensor_tensor(
                        dst, s, m, dst, ALU.mult, ALU.add), [u2] + dl2,
                        tag="mwc3")
                    R.wr(u3, f"mw{j}{g}")
                    R.rd(u3, "cB")
                    tmw[(j, g)] = u3
            for n in range(NS):
                par = n % 2
                bB = par * 2
                bC = 4 + par * 2
                tbs, tcs2 = [], []
                for qq in range(2):
                    tbb = mmg(PS(bB + qq, TT),
                              [(FAr(C_SEL + n * 128, 128, 0, 32),
                                FAr(C_BC + qq * TT, TT, 0, 32))],
                              deps=[t3] + R.rdep("csel")
                              + R.wdep(f"cps{bB + qq}"), tag="Bbc")
                    R.wr(tbb, f"cps{bB + qq}")
                    R.rd(tbb, "cbc")
                    tbs.append(tbb)
                for qq in range(2):
                    tcc2 = mmg(PS(bC + qq, TT),
                               [(FAr(C_SEL + (16 + n) * 128, 128, 0, 32),
                                 FAr(C_BC + qq * TT, TT, 0, 32))],
                               deps=[t3] + R.rdep("csel")
                               + R.wdep(f"cps{bC + qq}"), tag="Cbc")
                    R.wr(tcc2, f"cps{bC + qq}")
                    R.rd(tcc2, "cbc")
                    tcs2.append(tcc2)
                B_rep = ps8.ap()[:, bB * 512:bB * 512 + TC].unsqueeze(
                    1).broadcast_to([128, C_G, TC])
                C_rep = ps8.ap()[:, bC * 512:bC * 512 + TC].unsqueeze(
                    1).broadcast_to([128, C_G, TC])
                for g in range(2):
                    atoks = []
                    for q in range(C_G):
                        db = g * C_G + q
                        col = db * 16 + n
                        atoks.append(sca(lambda e, q=q, db=db, col=col:
                                         e.activation(
                            out=FAr(C_A + q * TC, TC),
                            in_=FAr(C_MW + db * TC, TC), func=ACT.Exp,
                            scale=ccol(c_ascale + col)),
                            R.wdep(f"cA{q}") + [tmw[(0, g)]]
                            + R.rdep("ascale"), tag="aexp"))
                    for q, t_ in enumerate(atoks):
                        R.rd(t_, f"mw0{g}")
                        R.wr(t_, f"cA{q}")
                    tb2 = vec(lambda e, g=g, br=B_rep: e.tensor_tensor(
                        FAr(C_B, C_G * TC).rearrange("p (d t) -> p d t", t=TC),
                        FAr(C_MW + 8192 + g * C_G * TC, C_G * TC).rearrange(
                            "p (d t) -> p d t", t=TC),
                        br, ALU.mult),
                        R.wdep("cB") + [tmw[(1, g)]] + tbs, tag="bmul")
                    R.wr(tb2, "cB")
                    R.rd(tb2, f"mw1{g}", f"cps{bB}", f"cps{bB + 1}")
                    swar = R.wdep("cH")
                    stoks = []
                    for q in range(C_G):
                        db = g * C_G + q
                        col = db * 16 + n
                        init = carry.ap()[:, col:col + 1] if c > 0 else 0.0
                        stoks.append(vec(lambda e, q=q, init=init:
                                         e.tensor_tensor_scan(
                            FAr(C_H + q * TC, TC), FAr(C_A + q * TC, TC),
                            FAr(C_B + q * TC, TC), init, ALU.mult, ALU.add),
                            swar + [atoks[q], tb2]
                            + (R.rdep("carry") if c > 0 else []), tag="scan"))
                    R.wr(stoks[-1], "cH")
                    R.rd(stoks[-1], "cB")
                    for q, t_ in enumerate(stoks):
                        R.rd(t_, f"cA{q}")
                        if q < C_G - 1:
                            R.rd(t_, "cB")
                    tcar = sca(lambda e, g=g, n=n: e.activation(
                        out=carry.ap()[:, :].rearrange(
                            "p (d s) -> p d s", s=16)[:, C_G * g:C_G * (g + 1),
                                                      n:n + 1],
                        in_=FAr(C_H, C_G * TC).rearrange(
                            "p (d t) -> p d t", t=TC)[:, :, TC - 1:TC],
                        func=ACT.Copy),
                        stoks + R.wdep("carry"), tag="car")
                    R["carry"].w = tcar
                    R.rd(tcar, "cH")
                    accws = R.wdep(f"cacc{g}") if n == 0 else []
                    if n == 0:
                        tac = vec(lambda e, g=g, cr=C_rep: e.tensor_tensor(
                            FAr(C_ACC + g * C_G * TC, C_G * TC).rearrange(
                                "p (d t) -> p d t", t=TC),
                            FAr(C_H, C_G * TC).rearrange(
                                "p (d t) -> p d t", t=TC),
                            cr, ALU.mult), stoks + tcs2 + accws, tag="hc0")
                        R.rd(tac, "cH")
                    else:
                        tmu = vec(lambda e, cr=C_rep: e.tensor_tensor(
                            FAr(C_H, C_G * TC).rearrange(
                                "p (d t) -> p d t", t=TC),
                            FAr(C_H, C_G * TC).rearrange(
                                "p (d t) -> p d t", t=TC),
                            cr, ALU.mult), stoks + [tcar] + tcs2, tag="hcm")
                        R.wr(tmu, "cH")
                        tac = vec(lambda e, g=g: e.tensor_add(
                            FAr(C_ACC + g * C_G * TC, C_G * TC),
                            FAr(C_ACC + g * C_G * TC, C_G * TC),
                            FAr(C_H, C_G * TC)),
                            [tmu] + R.wdep(f"cacc{g}"), tag="hca")
                        R.rd(tac, "cH")
                    R.rd(tac, f"cps{bC}", f"cps{bC + 1}")
                    R.wr(tac, f"cacc{g}")
            tys = []
            for g in range(2):
                tk = dma("gpsimd",
                         cc[f"y_in{l}_{c}"].rearrange(
                             "(d p) t -> p d t", p=128)[:, C_G * g:C_G * (g + 1), :],
                         FAr(C_ACC + g * C_G * TC, C_G * TC).rearrange(
                             "p (d t) -> p d t", t=TC),
                         deps=R.wdep(f"cacc{g}"), tag="yst")
                R.rd(tk, f"cacc{g}")
                tys.append(tk)
            ta = ag(cc[f"y_in{l}_{c}"][:, :],
                    cc[f"y_out{l}_{c}"][:, :], deps=tys, tag="yag")
            R.wr(ta, f"ag_y{l}_{c}")
            agy.append(ta)
        sch.barrier()

    def phaseD(l):
        Lw = LI[l]
        xin = x_ext if l == 0 else x1_s
        xout = x1_s if l == 0 else y_ext
        for kk in range(KDI):
            tk = dma("sync", FAr(D_WO + kk * 1024, 1024),
                     Lw["w_out"][128 * kk:128 * (kk + 1), :],
                     deps=R.wdep(f"dw{kk}"), tag="Dwo")
            R.wr(tk, f"dw{kk}")
        for it in range(NTILE):
            gchunks = [it // 2, 4 + it // 2]
            agdeps = [R[f"ag_y{l}_{cq}"].w for cq in gchunks]
            tyg = []
            for pb in range(KDI):
                msk0 = MSK_YL if pb < NDB else MSK_YU
                prow = (pb % NDB) * 128
                cands = []
                for (s, q) in ((0, 0), (0, 1), (1, 0), (1, 1)):
                    cch = q * 4 + it // 2
                    co2 = (it % 2) * TT
                    cands.append(cc[f"y_out{l}_{cch}"][s * DH + prow:s * DH + prow + 128,
                                                       co2:co2 + TT])
                y0 = FAr(D_T0, TT)
                y1 = FAr(D_T0 + 512, TT)
                tla = dma("sync", y0, cands[0], deps=agdeps + R.wdep("dy0"), tag="yl0")
                tc1 = vec(lambda e, y0=y0, m=mcol(msk0): e.tensor_scalar(
                    y0, y0, m, None, ALU.mult), [tla] + R.rdep("msk"),
                    tag="yc0")
                prev = tc1
                for ci in (1, 2, 3):
                    tlb = dma("sync", y1, cands[ci], deps=agdeps + R.wdep("dy1"),
                              tag="ylx")
                    R.wr(tlb, "dy1")
                    prev = vec(lambda e, y0=y0, y1=y1, m=mcol(msk0 + ci):
                               e.scalar_tensor_tensor(y0, y1, m, y0,
                                                      ALU.mult, ALU.add),
                               [prev, tlb], tag="ycx")
                    R.rd(prev, "dy1")
                R.wr(prev, "dy0")
                ut = FAr(D_T0 + 1024, TT)
                zt = FAr(D_T0 + 1536, TT)
                tu = dma("sync", ut, u_s[128 * pb:128 * (pb + 1), it * TT:(it + 1) * TT],
                         deps=R.wdep("du"), tag="dul")
                R.wr(tu, "du")
                tz2 = dma("sync", zt, z_s[128 * pb:128 * (pb + 1), it * TT:(it + 1) * TT],
                          deps=R.wdep("dz"), tag="dzl")
                R.wr(tz2, "dz")
                ty2 = vec(lambda e, ut=ut, y0=y0, pb=pb: e.scalar_tensor_tensor(
                    y0, ut, ccol(c_dcol + pb), y0, ALU.mult, ALU.add),
                    [prev, tu] + R.rdep("dcol"), tag="yd")
                R.rd(ty2, "du")
                R.wr(ty2, "dy0")
                tsz = sca(lambda e, zt=zt: e.activation(out=zt, in_=zt, func=ACT.Silu),
                          [tz2], tag="dsilu")
                R.wr(tsz, "dz")
                tyg2 = vec(lambda e, pb=pb, y0=y0, zt=zt: e.tensor_mul(
                    FAr(D_YG + pb * TT, TT), y0, zt),
                    [ty2, tsz] + R.wdep(f"dyg{pb}"), tag="ygm")
                R.rd(tyg2, "dy0", "dz")
                R.wr(tyg2, f"dyg{pb}")
                tyg.append(tyg2)
            for mm in range(KDM):
                bank = mm % 4
                grp = mmg(PS(bank, TT),
                          [(FAr(D_WO + kk * 1024 + mm * 128, 128),
                            FAr(D_YG + kk * TT, TT)) for kk in range(KDI)],
                          deps=tyg + [R[f"dw{kk}"].w for kk in range(KDI)]
                          + R.wdep(f"ps{bank}"), tag="omm")
                R.wr(grp, f"ps{bank}")
                for pb in range(KDI):
                    R.rd(grp, f"dyg{pb}")
                so = FAr(D_T0 + 2048, TT)
                if l == 0:
                    txl = dma("sync", FAr(D_T0 + 2560, TT),
                              xn_s[128 * mm:128 * (mm + 1), it * TT:(it + 1) * TT],
                              deps=R.wdep("dxn"), tag="dxnl")
                    R.wr(txl, "dxn")
                    tso = vec(lambda e, bank=bank, so=so: e.tensor_add(
                        so, PS(bank, TT), FAr(D_T0 + 2560, TT)),
                        [grp, txl] + R.wdep("dso"), tag="sadd")
                    R.rd(tso, f"ps{bank}", "dxn")
                else:
                    tso = vec(lambda e, bank=bank, so=so: e.tensor_copy(
                        so, PS(bank, TT)), [grp] + R.wdep("dso"), tag="scp")
                    R.rd(tso, f"ps{bank}")
                R.wr(tso, "dso")
                tgl = dma("sync", FAr(D_T0 + 3072, TT),
                          g_s[128 * mm:128 * (mm + 1), it * TT:(it + 1) * TT],
                          deps=R.wdep("dg"), tag="dgl")
                R.wr(tgl, "dg")
                tmu = vec(lambda e, so=so, b=FAr(D_T0 + 3072, TT): e.tensor_mul(so, so, b),
                          [tso, tgl], tag="sgm")
                R.rd(tmu, "dg")
                R.wr(tmu, "dso")
                txl2 = dma("sync", FAr(D_T0 + 3584, TT),
                           xin[128 * mm:128 * (mm + 1), 3 + it * TT:3 + (it + 1) * TT],
                           deps=R.wdep("dx"), tag="dxl")
                R.wr(txl2, "dx")
                tfin = vec(lambda e, so=so, b=FAr(D_T0 + 3584, TT): e.tensor_add(so, so, b),
                           [tmu, txl2], tag="xadd")
                R.rd(tfin, "dx")
                R.wr(tfin, "dso")
                if l == 0:
                    tk = dma("gpsimd", x1_s[128 * mm:128 * (mm + 1),
                                            3 + it * TT:3 + (it + 1) * TT],
                             so, deps=[tfin], tag="x1st")
                else:
                    tk = dma("gpsimd", y_ext[128 * mm:128 * (mm + 1),
                                             it * TT:(it + 1) * TT],
                             so, deps=[tfin], tag="yout")
                R.rd(tk, "dso")
                R.wr(tk, f"dout{mm}_{it}")
        if l == 0:
            douts = [R[f"dout{mm}_{NTILE - 1}"].w for mm in range(KDM)]
            tki = dma("gpsimd", cc["xt_in"][:, :], x1_s[:, TPC:TPC + 3],
                      deps=douts, tag="xtst")
            ta = ag(cc["xt_in"][:, :], cc["xt_out"][:, :], deps=[tki], tag="xtag")
            for pb in range(KDM):
                h0 = FAr(D_T0, 3)
                h1 = FAr(D_T0 + 8, 3)
                tla = dma("sync", h0, cc["xt_out"][128 * pb:128 * (pb + 1), :],
                          deps=[ta] + R.wdep("dy0"), tag="xtl0")
                tlb = dma("sync", h1, cc["xt_out"][DM + 128 * pb:DM + 128 * (pb + 1), :],
                          deps=[ta] + R.wdep("dy1"), tag="xtl1")
                R.wr(tlb, "dy1")
                t1 = vec(lambda e, h0=h0: e.tensor_scalar(
                    h0, h0, mcol(MSK_XH), None, ALU.mult), [tla] + R.rdep("msk"),
                    tag="xtc0")
                t2 = vec(lambda e, h0=h0, h1=h1: e.scalar_tensor_tensor(
                    h0, h1, mcol(MSK_XH + 1), h0, ALU.mult, ALU.add), [t1, tlb],
                    tag="xtc1")
                R.rd(t2, "dy1")
                R.wr(t2, "dy0")
                tk = dma("gpsimd", x1_s[128 * pb:128 * (pb + 1), 0:3], h0,
                         deps=[t2], tag="xhst")
                R.rd(tk, "dy0")
        sch.barrier()

    for l in range(2):
        phaseAx(l)
        t1 = ag(cc[f"m_in{l}"][:, :], cc[f"m_out{l}"][:, :], deps=sch.barrier(),
                tag="mag")
        R.wr(t1, f"ag_m{l}")
        t2 = ag(cc[f"w_in{l}"][:, :], cc[f"w_out{l}"][:, :], deps=[t1], tag="wag")
        R.wr(t2, f"ag_w{l}")
        t3 = ag(cc[f"bc_in{l}"][:, :], cc[f"bc_out{l}"][:, :], deps=[t2], tag="bcag")
        R.wr(t3, f"ag_bc{l}")
        phaseAz(l)
        use_bphase()
        if l == 0:
            phaseB0()
        else:
            phaseB1()
        use_big()
        phaseC(l)
        phaseD(l)

    sch.barrier()

    NPOOL = 12
    sems = {}
    dma_sems = {}
    semcms = []
    for e in ENGINES:
        cm = nc.semaphore(f"S_{e}")
        sems[e] = cm.__enter__()
        semcms.append(cm)
        dma_sems[e] = []
        for i in range(NPOOL):
            cm2 = nc.semaphore(f"D_{e}{i}")
            dma_sems[e].append(cm2.__enter__())
            semcms.append(cm2)
    blkcm = nc.Block()
    block = blkcm.__enter__()
    sch.finalize(block, sems, dma_sems, NPOOL)
    blkcm.__exit__(None, None, None)
    for cm in reversed(semcms):
        cm.__exit__(None, None, None)
    for cm in reversed(arena["cms"]):
        cm.__exit__(None, None, None)
    for cm in reversed(keep):
        cm.__exit__(None, None, None)
    nops = {e: sch.nseq[e] for e in ENGINES}
    return nc, nops



def _prep_core_inputs(inputs, core):
    p = core % 2
    b = core // 2
    perm = np.arange(DI) if p == 0 else np.concatenate(
        [np.arange(DH, DI), np.arange(0, DH)])
    f = np.float32
    d = {}
    x = np.asarray(inputs["L0_attn_x"], dtype=f)
    tok0 = p * TPC
    halo = np.zeros((3, DM), f) if p == 0 else x[b, tok0 - 3:tok0]
    xc = np.concatenate([halo, x[b, tok0:tok0 + TPC]], axis=0)
    d["x"] = np.ascontiguousarray(xc.T)

    w_in = np.asarray(inputs["L0_attn_W_in"], f)
    d["l0_w_in_x"] = np.ascontiguousarray(w_in[:, :DI][:, perm])
    d["l0_w_in_z"] = np.ascontiguousarray(w_in[:, DI:][:, perm])
    d["l0_conv_w"] = np.ascontiguousarray(
        np.asarray(inputs["L0_attn_conv_w"], f)[:, 0, :][perm])
    d["l0_conv_b"] = np.ascontiguousarray(np.asarray(inputs["L0_attn_conv_b"], f)[perm])
    d["l0_w_xproj"] = np.ascontiguousarray(np.asarray(inputs["L0_attn_W_xproj"], f)[perm])
    d["l0_w_dt"] = np.ascontiguousarray(np.asarray(inputs["L0_attn_W_dt"], f).ravel()[perm])
    d["l0_b_dt"] = np.ascontiguousarray(np.asarray(inputs["L0_attn_b_dt"], f)[perm])
    d["l0_a_log"] = np.ascontiguousarray(np.asarray(inputs["L0_attn_A_log"], f)[perm])
    d["l0_d"] = np.ascontiguousarray(np.asarray(inputs["L0_attn_D"], f)[perm])
    d["l0_w_out"] = np.ascontiguousarray(np.asarray(inputs["L0_attn_W_out"], f)[perm])
    d["l0_norm_w"] = np.ascontiguousarray(np.asarray(inputs["L0_attn_norm_w"], f))
    d["l0_w_gate"] = np.ascontiguousarray(np.asarray(inputs["L0_ffn_w_gate"], f).T)
    d["l0_w_up"] = np.ascontiguousarray(np.asarray(inputs["L0_ffn_w_up"], f).T)
    d["l0_w_down"] = np.ascontiguousarray(np.asarray(inputs["L0_ffn_w_down"], f).T)

    ipw = np.asarray(inputs["L1_attn_in_proj_w"], f)
    d["l1_w_in_x"] = np.ascontiguousarray(ipw[:DI][perm].T)
    d["l1_w_in_z"] = np.ascontiguousarray(ipw[DI:][perm].T)
    d["l1_conv_w"] = np.ascontiguousarray(
        np.asarray(inputs["L1_attn_conv_w"], f)[:, 0, :][perm])
    d["l1_conv_b"] = np.ascontiguousarray(np.asarray(inputs["L1_attn_conv_b"], f)[perm])
    d["l1_w_xproj"] = np.ascontiguousarray(
        np.asarray(inputs["L1_attn_x_proj_w"], f)[:, perm].T)
    d["l1_w_dtproj"] = np.ascontiguousarray(
        np.asarray(inputs["L1_attn_dt_proj_w"], f)[perm].T)
    d["l1_b_dt"] = np.ascontiguousarray(np.asarray(inputs["L1_attn_dt_proj_b"], f)[perm])
    d["l1_a_log"] = np.ascontiguousarray(np.asarray(inputs["L1_attn_A_log"], f)[perm])
    d["l1_d"] = np.ascontiguousarray(np.asarray(inputs["L1_attn_D"], f)[perm])
    d["l1_w_out"] = np.ascontiguousarray(
        np.asarray(inputs["L1_attn_out_proj_w"], f)[:, perm].T)
    d["l1_w1"] = np.ascontiguousarray(np.asarray(inputs["L1_ffn_W1"], f).T)
    d["l1_b1"] = np.ascontiguousarray(np.asarray(inputs["L1_ffn_b1"], f))
    d["l1_w2"] = np.ascontiguousarray(np.asarray(inputs["L1_ffn_W2"], f).T)
    d["l1_b2"] = np.ascontiguousarray(np.asarray(inputs["L1_ffn_b2"], f))

    msk = np.zeros((128, NMSK), f)
    for c in range(NCHUNK):
        owner = c // 4
        if owner == p:
            msk[:, MSK_MW + 3 * c + 0] = 1.0
        elif owner == 0:
            msk[:, MSK_MW + 3 * c + 1] = 1.0
        else:
            msk[:, MSK_MW + 3 * c + 2] = 1.0
    msk[:, MSK_YL + 2 * p + p] = 1.0
    msk[:, MSK_YU + 2 * (1 - p) + p] = 1.0
    if p == 1:
        msk[:, MSK_XH + 0] = 1.0
    d["msk"] = msk
    d["eye"] = np.eye(128, dtype=f)
    seln = np.zeros((32, 32 * 128), f)
    for n in range(32):
        seln[n, n * 128:(n + 1) * 128] = 1.0
    d["seln"] = seln
    return d


_CACHE = {}


def kernel(**inputs):
    if "nc" not in _CACHE:
        nc, nops = build_program()
        _CACHE["nc"] = nc
    nc = _CACHE["nc"]
    in_maps = [_prep_core_inputs(inputs, c) for c in range(8)]
    res = run_bass_kernel_spmd(nc, in_maps, core_ids=list(range(8)))
    out = np.zeros((4, SEQ, DM), dtype=np.float32)
    for c in range(8):
        p = c % 2
        b = c // 2
        out[b, p * TPC:(p + 1) * TPC, :] = res.results[c]["y"].T
    return out


# version 06 (final)
# speedup vs baseline: 1.0161x; 1.0161x over previous
import os

import numpy as np

import concourse.bass as bass
import concourse.mybir as mybir
from concourse.bass_utils import run_bass_kernel_spmd

F32 = mybir.dt.float32
BF16 = mybir.dt.bfloat16
I32 = mybir.dt.int32
ALU = mybir.AluOpType
ACT = mybir.ActivationFunctionType

DM = 1024
DI = 2048
DH = DI // 2
NS = 16
SEQ = 8192
TPC = 4096
TT = 512
NTILE = TPC // TT
TC = 1024
NCHUNK = SEQ // TC
KDM = DM // 128
KDI = DI // 128
MI = 4096
KMI = MI // 128
NDB = DH // 128
EPS_LN = 1e-5
EPS_RMS = float(np.finfo(np.float32).eps)
GROUPS = [[0, 1], [2, 3], [4, 5], [6, 7]]
ENGINES = ("tensor", "vector", "scalar", "gpsimd", "sync")
_LAST_SCH = [None]

MSK_MW = 0
MSK_YL = 24
MSK_YU = 28
MSK_XH = 32
NMSK = 40


class Sch:
    def __init__(self):
        self.ops = {e: [] for e in ENGINES}
        self.nseq = {e: 0 for e in ENGINES}
        self.ndma = {e: 0 for e in ENGINES}
        self.awaited = {e: set() for e in ENGINES}
        self.snapshot_fn = None
        self.restore_fn = None

    def op(self, engine, emit, deps=(), is_dma=False, tag=""):
        myseq = self.nseq[engine]
        dd = []
        same_max = -1
        for d in deps:
            if d is None:
                continue
            (de, ds, ddma) = d
            if ddma == "ALL":
                key = (de, ds, "ALL")
                if key not in dd:
                    dd.append(key)
                continue
            if de == engine and not ddma:
                same_max = max(same_max, ds)
                continue
            key = (de, ds, ddma)
            if key not in dd:
                dd.append(key)
            if not ddma:
                self.awaited[de].add(ds)
        self.ops[engine].append(dict(
            emit=emit, deps=dd, seq=myseq, is_dma=is_dma, tag=tag,
            same_max=same_max,
            snap=self.snapshot_fn() if self.snapshot_fn else None))
        self.nseq[engine] += 1
        if is_dma:
            self.ndma[engine] += 1
            return (engine, self.ndma[engine], True)
        return (engine, myseq, False)

    def last(self, engine):
        if self.nseq[engine] == 0:
            return None
        rec = self.ops[engine][-1]
        if rec["is_dma"]:
            return (engine, self.ndma[engine], True)
        return (engine, rec["seq"], False)

    def barrier(self):
        lasts = [self.last(e) for e in ENGINES]
        alldma = [(e2, self.ndma[e2], "ALL") for e2 in ENGINES if self.ndma[e2]]
        toks = []
        for e in ENGINES:
            deps = [t for t in lasts if t is not None and t[0] != e and not t[2]]
            deps = deps + alldma
            toks.append(self.op(e, lambda eng: eng.nop(), deps, tag="barrier"))
        return toks

    def finalize(self, block, sems, dma_sems, npool):
        incval = {}
        for e in ENGINES:
            aw = sorted(self.awaited[e])
            incval[e] = {s: i + 1 for i, s in enumerate(aw)}

        def dma_sem_val(de, di):
            return dma_sems[de][(di - 1) % npool], 16 * ((di - 1) // npool + 1)

        def run_engine(eng_obj, e):
            hi = {}
            ndma_issued = 0
            drained_upto = -1
            for rec in self.ops[e]:
                if rec["same_max"] > drained_upto:
                    eng_obj.drain()
                    drained_upto = rec["seq"] - 1
                for (de, ds, ddma) in rec["deps"]:
                    if ddma == "ALL":
                        for s in range(min(ds, npool)):
                            sem = dma_sems[de][s]
                            val = 16 * ((ds - 1 - s) // npool + 1)
                            k = id(sem)
                            if hi.get(k, -1) >= val:
                                continue
                            eng_obj.wait_ge(sem, val)
                            hi[k] = val
                        continue
                    if ddma:
                        sem, val = dma_sem_val(de, ds)
                    else:
                        sem, val = sems[de], incval[de][ds]
                    k = id(sem)
                    if hi.get(k, -1) >= val:
                        continue
                    eng_obj.wait_ge(sem, val)
                    hi[k] = val
                if rec["is_dma"]:
                    di = ndma_issued + 1
                    sem, val = dma_sem_val(e, di)
                    if di > npool:
                        pv = 16 * ((di - 1) // npool)
                        k = id(sem)
                        if hi.get(k, -1) < pv:
                            eng_obj.wait_ge(sem, pv)
                            hi[k] = pv
                    if rec["snap"] is not None:
                        self.restore_fn(rec["snap"])
                    ins = rec["emit"](eng_obj)
                    ins.then_inc(sem, 16)
                    ndma_issued += 1
                else:
                    if rec["snap"] is not None:
                        self.restore_fn(rec["snap"])
                    ins = rec["emit"](eng_obj)
                    if rec["seq"] in incval[e]:
                        if ins is None:
                            raise RuntimeError("awaited op returned None: " + rec["tag"])
                        ins.then_inc(sems[e], 1)

        for e in ENGINES:
            getattr(block, e)(lambda eng, e=e: run_engine(eng, e))


class RT:
    __slots__ = ("w", "r")

    def __init__(self):
        self.w = None
        self.r = []


class Regs:
    def __init__(self):
        self.d = {}

    def __getitem__(self, k):
        if k not in self.d:
            self.d[k] = RT()
        return self.d[k]

    def rdep(self, *ks):
        return [self[k].w for k in ks if self[k].w is not None]

    def rd(self, tok, *ks):
        for k in ks:
            self[k].r.append(tok)

    def wdep(self, *ks):
        out = []
        for k in ks:
            r = self[k]
            if r.w is not None:
                out.append(r.w)
            out.extend(r.r)
        return out

    def wr(self, tok, *ks):
        for k in ks:
            r = self[k]
            r.w = tok
            r.r = []


A_WX = 0
A_XT = 16384
A_XN = 20480
A_SQ = 24576
A_U = 28672
A_CIN = 29696
A_ZE = 30736
A_XD = 31760
A_DT = 32272
A_WW = 33808
A_TK = 34320
RT_KF = 34368
RT_D = 34880
FA_N = 44032
C_MW = 0
C_ACC = 16384
C_A = 24576
C_B = 28672
C_H = 32768
C_BC = 36864
C_SEL = 37888
C_G = 4
S0_W0 = 0
S0_W1 = 4096
S0_XN = 12800
B_XN = 0
B_T0 = 4096
B_SB5 = 7168
B_ABS = 7680
B_RT = 11776
FB_N = 19456
D_WO = 0
D_YG = 16384
D_T0 = 24576


def build_program(dev=()):
    nc = bass.Bass("TRN2", target_bir_lowering=False)
    sch = Sch()
    _LAST_SCH[0] = sch
    dev = set(dev)
    R = Regs()

    def din(name, shape, dtype=F32):
        return nc.dram_tensor(name, list(shape), dtype, kind="ExternalInput").ap()

    def dscr(name, shape, dtype=F32):
        kind = "ExternalOutput" if name in dev else "Internal"
        return nc.dram_tensor(name, list(shape), dtype, kind=kind).ap()

    x_ext = din("x", (DM, TPC + 3))
    LI = [{}, {}]
    for i in range(2):
        LI[i]["w_in_x"] = din(f"l{i}_w_in_x", (DM, DI))
        LI[i]["w_in_z"] = din(f"l{i}_w_in_z", (DM, DI))
        LI[i]["conv_w"] = din(f"l{i}_conv_w", (DI, 4))
        LI[i]["conv_b"] = din(f"l{i}_conv_b", (DI,))
        LI[i]["a_log"] = din(f"l{i}_a_log", (DI, NS))
        LI[i]["d"] = din(f"l{i}_d", (DI,))
        LI[i]["w_out"] = din(f"l{i}_w_out", (DI, DM))
        LI[i]["b_dt"] = din(f"l{i}_b_dt", (DI,))
    LI[0]["w_xproj"] = din("l0_w_xproj", (DI, 33))
    LI[0]["w_dt"] = din("l0_w_dt", (DI,))
    LI[0]["norm_w"] = din("l0_norm_w", (DM,))
    LI[0]["wg"] = din("l0_w_gate", (DM, MI))
    LI[0]["wu"] = din("l0_w_up", (DM, MI))
    LI[0]["wd"] = din("l0_w_down", (MI, DM))
    LI[1]["w_xproj"] = din("l1_w_xproj", (DI, 96))
    LI[1]["w_dtp"] = din("l1_w_dtproj", (64, DI))
    LI[1]["w1"] = din("l1_w1", (DM, MI))
    LI[1]["b1"] = din("l1_b1", (MI,))
    LI[1]["w2"] = din("l1_w2", (MI, DM))
    LI[1]["b2"] = din("l1_b2", (DM,))
    msk_in = din("msk", (128, NMSK))
    eye_in = din("eye", (128, 128))
    seln_in = din("seln", (32, 32 * 128))
    y_ext = nc.dram_tensor("y", [DM, TPC], F32, kind="ExternalOutput").ap()

    xn_s = dscr("xn_s", (DM, TPC))
    xn2_s = dscr("xn2_s", (DM, TPC))
    u_s = dscr("u_s", (DI, TPC))
    z_s = dscr("z_s", (DI, TPC))
    g_s = dscr("g_s", (DM, TPC))
    sg_s = dscr("sg_s", (MI, TPC))
    gu_s = dscr("gu_s", (MI, TPC))
    xq_s = dscr("xq_s", (DM, TPC), BF16)
    srow_s = dscr("srow_s", (4, TPC))
    h1hi_s = dscr("h1hi_s", (MI, TPC), BF16)
    h1lo_s = dscr("h1lo_s", (MI, TPC), BF16)
    x1_s = dscr("x1_s", (DM, TPC + 3))
    m_loc = dscr("m_loc", (DH, TPC))
    w_loc = dscr("w_loc", (DH, TPC))
    wq_sd = {
        "g": dscr("wqg_s", (DM, MI), BF16),
        "u": dscr("wqu_s", (DM, MI), BF16),
        "d": dscr("wqd_s", (MI, DM), BF16),
        "w1": dscr("wt1_s", (DM, MI), BF16),
        "w2": dscr("wt2_s", (MI, DM), BF16),
    }
    cc = {}
    for i in range(2):
        cc[f"m_in{i}"] = dscr(f"cc_m_in{i}", (DH, TPC))
        cc[f"m_out{i}"] = dscr(f"cc_m_out{i}", (2 * DH, TPC))
        cc[f"w_in{i}"] = dscr(f"cc_w_in{i}", (DH, TPC))
        cc[f"w_out{i}"] = dscr(f"cc_w_out{i}", (2 * DH, TPC))
        cc[f"bc_in{i}"] = dscr(f"cc_bc_in{i}", (32, TPC))
        cc[f"bc_out{i}"] = dscr(f"cc_bc_out{i}", (64, TPC))
        for cch in range(NCHUNK):
            cc[f"y_in{i}_{cch}"] = dscr(f"cc_y_in{i}_{cch}", (DH, TC))
            cc[f"y_out{i}_{cch}"] = dscr(f"cc_y_out{i}_{cch}", (2 * DH, TC))
    cc["xt_in"] = dscr("cc_xt_in", (DM, 3))
    cc["xt_out"] = dscr("cc_xt_out", (2 * DM, 3))

    keep = []

    def sb(name, shape, dtype=F32):
        cm = nc.sbuf_tensor(name, list(shape), dtype)
        h = cm.__enter__()
        keep.append(cm)
        return h

    arena = {"cms": [], "fa": None, "wb": None, "ba": None, "n": 0,
             "rt": (RT_KF, RT_D)}

    def use_big():
        for cm in reversed(arena["cms"]):
            cm.__exit__(None, None, None)
        arena["n"] += 1
        cm = nc.sbuf_tensor(f"FAb{arena['n']}", [128, FA_N], F32)
        arena["cms"] = [cm]
        arena["fa"] = cm.__enter__()
        arena["wb"] = None
        arena["ba"] = None
        arena["rt"] = (RT_KF, RT_D)

    def use_bphase():
        for cm in reversed(arena["cms"]):
            cm.__exit__(None, None, None)
        arena["n"] += 1
        cm1 = nc.sbuf_tensor(f"FBs{arena['n']}", [128, FB_N], F32)
        cm2 = nc.sbuf_tensor(f"WBw{arena['n']}", [128, 32768], BF16)
        cm3 = nc.sbuf_tensor(f"BAq{arena['n']}", [128, 16390], BF16)
        arena["cms"] = [cm1, cm2, cm3]
        arena["fa"] = cm1.__enter__()
        arena["wb"] = cm2.__enter__()
        arena["ba"] = cm3.__enter__()
        arena["rt"] = (B_RT, B_RT + 512)

    sch.snapshot_fn = lambda: (arena["fa"], arena["wb"], arena["ba"],
                               arena["rt"])
    sch.restore_fn = lambda s: arena.update(fa=s[0], wb=s[1], ba=s[2],
                                            rt=s[3])
    IA = sb("IA", (128, 512), I32)
    cons = sb("cons", (128, 512))
    srow = sb("srow", (128, 4352))
    wdtp = sb("wdtp", (64, 2048))
    carry = sb("carry", (128, 128))
    eye = sb("eye_sb", (128, 128))
    cmP = nc.psum_tensor("ps8", [128, 4096], F32)
    ps8 = cmP.__enter__()
    keep.append(cmP)
    use_bphase()

    def FAr(off, n, p0=0, p1=128):
        return arena["fa"].ap()[p0:p1, off:off + n]

    def BAr(off, n, p0=0, p1=128):
        return arena["ba"].ap()[p0:p1, off:off + n]

    def WQv(off, n):
        return arena["wb"].ap()[:, off:off + n]

    def PS(bank, n=512, off=0, p0=0, p1=128):
        return ps8.ap()[p0:p1, bank * 512 + off:bank * 512 + off + n]

    c_ones = 0
    c_onesb = 1
    c_msk = 4
    c_convw = 44
    c_convb = 108
    c_dtw = 124
    c_dtb = 140
    c_dcol = 156
    c_nw = 172
    c_ascale = 180
    c_b1 = 308
    c_b2 = 340
    c_csg = 348
    c_csu = 380
    c_csd = 412
    c_kap = 420

    SEL0 = 0
    R_Q = 128
    R_STAT = 140
    R_PMT = 2188
    R_PMS = 2192
    R_BR = 2320
    R_ALOG = 4096

    def dma(q, dst, src, deps=(), tag="", slow=False):
        return sch.op(q, lambda e, d=dst, s=src, sl=slow: e.dma_start(
            out=d, in_=s, allow_slow_non_contiguous=sl),
            deps, is_dma=True, tag=tag)

    def vec(fn, deps=(), tag=""):
        return sch.op("vector", fn, deps, tag=tag)

    def sca(fn, deps=(), tag=""):
        return sch.op("scalar", fn, deps, tag=tag)

    def ten(fn, deps=(), tag=""):
        return sch.op("tensor", fn, deps, tag=tag)

    def mmg(out_ap, pairs, deps=(), tag="mm"):
        def emit(eng, pairs=tuple(pairs), out_ap=out_ap, n=len(pairs)):
            ins = None
            for i, (l, r) in enumerate(pairs):
                ins = eng.matmul(out_ap, l, r, start=(i == 0), stop=(i == n - 1))
            return ins
        return sch.op("tensor", emit, deps, tag=tag)

    def ag(in_ap, out_ap, deps=(), tag="ag"):
        def emit(eng, a=in_ap, b=out_ap):
            return eng.collective_compute("AllGather", ALU.bypass,
                                          replica_groups=GROUPS, ins=[a], outs=[b])
        return sch.op("gpsimd", emit, deps, tag=tag)

    ONES1 = srow.ap()[0:1, SEL0:SEL0 + 128]
    ONESC = cons.ap()[:, c_ones:c_ones + 1]
    def ONESB():
        return arena["ba"].ap()[:, 16384:16385]

    def mcol(i):
        return cons.ap()[:, c_msk + i:c_msk + i + 1]

    def ccol(c0, n=1):
        return cons.ap()[:, c0:c0 + n]

    t = vec(lambda e: e.memset(ONESC, 1.0), tag="i1")
    R.wr(t, "cons")
    t = vec(lambda e: e.memset(ONESB(), 1.0), tag="i2")
    R.wr(t, "onesb")
    t = vec(lambda e: e.memset(srow.ap()[0:1, SEL0:SEL0 + 128], 1.0), tag="i3")
    R.wr(t, "ones1")
    t = dma("sync", cons.ap()[:, c_msk:c_msk + NMSK], msk_in[:, :], tag="imsk")
    R.wr(t, "msk")
    t = dma("sync", eye.ap()[:, :], eye_in[:, :], tag="ieye")
    R.wr(t, "eye")

    def round_p1(v_ap, out_ap, deps, lo, hi):
        n = v_ap.shape[-1]
        ki = IA.ap()[:, 0:n]
        kf = FAr(arena["rt"][0], n)
        dd2 = FAr(arena["rt"][1], n)
        d0 = list(deps) + R.wdep("rt")
        t1 = vec(lambda e: e.tensor_copy(ki, v_ap), d0, tag="f2i")
        t2 = vec(lambda e: e.tensor_copy(kf, ki), [t1])
        t3 = vec(lambda e: e.tensor_sub(dd2, v_ap, kf), [t2])
        t4 = vec(lambda e: e.scalar_tensor_tensor(kf, dd2, 0.5, kf, ALU.is_gt, ALU.add), [t3])
        t5 = vec(lambda e: e.scalar_tensor_tensor(kf, dd2, -0.5, kf, ALU.is_ge, ALU.add), [t4])
        t6 = vec(lambda e: e.tensor_scalar(out_ap, kf, float(lo), float(hi),
                                           ALU.max, ALU.min), [t5])
        R.wr(t6, "rt")
        return t6

    def pmax_row(src_ap, row_off, deps, tag="pm"):
        toks = []
        tmp = srow.ap()[:, R_PMT:R_PMT + 4]
        d0 = list(deps) + R.wdep("pm") + R.rdep("eye")
        for q in range(4):
            pt = ten(lambda e, q=q: e.transpose(
                PS(7, 128, 0), src_ap[:, q * 128:(q + 1) * 128], eye.ap()[:, :]),
                d0 + toks[-1:], tag=tag + "t")
            toks.append(vec(lambda e, q=q: e.tensor_reduce(
                tmp[:, q:q + 1], PS(7, 128, 0), mybir.AxisListType.X, ALU.max,
                apply_absolute_value=True), [pt], tag=tag + "r"))
        t2 = ten(lambda e: e.transpose(PS(7, 128, 128, 0, 4), tmp, eye.ap()[:, :]),
                 toks, tag=tag + "tb")
        sb4 = srow.ap()[0:4, R_PMS:R_PMS + 128]
        t3 = vec(lambda e: e.tensor_copy(sb4, PS(7, 128, 128, 0, 4)), [t2])
        t4 = dma("sync", srow.ap()[0:1, row_off:row_off + 512], sb4,
                 deps=[t3] + R.wdep("brows"), tag=tag + "d")
        R.wr(t4, "pm")
        R.wr(t4, "brows")
        return t4

    def s0_quant(src, dst, K, M, krow, lo_clip, cs_col, nmblk, tag):
        nK = K // 128
        absacc = srow.ap()[:, R_Q:R_Q + 1]
        tz = vec(lambda e: e.memset(absacc, 0.0), R.rdep("cons"), tag=tag + "z")
        tacc = tz
        for kk in range(nK):
            woff = S0_W0 if kk % 2 == 0 else S0_W1
            wt = FAr(woff, M)
            tl = dma("sync", wt, src[128 * kk:128 * (kk + 1), :],
                     deps=R.wdep(f"qw{kk % 2}"), tag=tag + "ld")
            R.wr(tl, f"qw{kk % 2}")
            tmpc = srow.ap()[:, R_Q + 1:R_Q + 2]
            ta = sca(lambda e, w=wt, t2=tmpc: e.activation(
                out=FAr(S0_XN, M), in_=w, func=ACT.Abs, accum_out=t2),
                [tl, tacc] + R.wdep("qv"), tag=tag + "abs")
            R.rd(ta, f"qw{kk % 2}")
            R.wr(ta, "qv")
            tacc = vec(lambda e, t2=tmpc: e.tensor_add(absacc, absacc, t2), [ta])
        tsum = mmg(PS(4, 1, 0, 0, 1), [(absacc, ONESC)],
                   deps=[tacc] + R.rdep("cons"), tag=tag + "sum")
        kcell = cons.ap()[0:1, c_kap + krow:c_kap + krow + 1]
        kinv_c = srow.ap()[0:1, R_Q + 2:R_Q + 3]
        t1 = vec(lambda e: e.tensor_scalar(kcell, PS(4, 1, 0, 0, 1), 1.0 / (K * M),
                                           lo_clip, ALU.mult, ALU.max), [tsum])
        R.rd(t1, "cons")
        t2 = vec(lambda e: e.reciprocal(kinv_c, kcell), [t1])
        tb = mmg(PS(4, 1, 4), [(ONES1, kinv_c)], deps=[t2] + R.rdep("ones1"),
                 tag=tag + "kb")
        kinv_col = srow.ap()[:, R_Q + 4:R_Q + 5]
        t3 = vec(lambda e: e.tensor_copy(kinv_col, PS(4, 1, 4)), [tb])
        tcs = None
        if cs_col is not None:
            tcs = vec(lambda e: e.memset(ccol(cs_col, nmblk), 0.0),
                      [t3] + R.wdep("cons"), tag=tag + "cs0")
        prevq = [t3]
        for kk in range(nK):
            woff = S0_W0 if kk % 2 == 0 else S0_W1
            wt = FAr(woff, M)
            tl = dma("sync", wt, src[128 * kk:128 * (kk + 1), :],
                     deps=prevq + R.wdep(f"qw{kk % 2}"), tag=tag + "l2")
            R.wr(tl, f"qw{kk % 2}")
            tv = vec(lambda e, w=wt, o=FAr(S0_XN, M): e.tensor_scalar(o, w, kinv_col, None,
                                                     ALU.mult),
                     [tl, t3] + R.wdep("qv"), tag=tag + "v")
            R.rd(tv, f"qw{kk % 2}")
            R.wr(tv, "qv")
            tchunk = tv
            for q0 in range(0, M, 512):
                tr = round_p1(FAr(S0_XN + q0, 512), BAr(q0, 512),
                              [tchunk] + (R.wdep("qba") if q0 == 0 else []), 0.0, 2.0)
                tchunk = vec(lambda e, b=BAr(q0, 512): e.tensor_scalar(
                    b, b, 1.0, None, ALU.subtract), [tr], tag=tag + "sh")
            R.rd(tchunk, "qv")
            R.wr(tchunk, "qba")
            tst = dma("gpsimd", dst[128 * kk:128 * (kk + 1), :], BAr(0, M),
                      deps=[tchunk], tag=tag + "st")
            R.rd(tst, "qba")
            prevq = [tst]
            if cs_col is not None:
                tprev = tcs
                for mm in range(nmblk):
                    tm = mmg(PS(5, 1), [(BAr(mm * 128, 128), ONESB())],
                             deps=[tchunk, tprev] + R.rdep("onesb"), tag=tag + "cs")
                    R.rd(tm, "qba")
                    tprev = vec(lambda e, mm=mm: e.tensor_add(
                        ccol(cs_col + mm), ccol(cs_col + mm), PS(5, 1)), [tm])
                tcs = tprev
                prevq.append(tprev)
        if cs_col is not None:
            R.wr(tcs, "cons2")
        return prevq

    s0_quant(LI[0]["wg"], wq_sd["g"], DM, MI, 0, 1e-5, c_csg, KMI, "qg")
    s0_quant(LI[0]["wu"], wq_sd["u"], DM, MI, 1, 1e-5, c_csu, KMI, "qu")
    s0_quant(LI[0]["wd"], wq_sd["d"], MI, DM, 2, 1e-5, c_csd, KDM, "qd")
    s0_quant(LI[1]["w1"], wq_sd["w1"], DM, MI, 3, 1e-8, None, 0, "q1")
    s0_quant(LI[1]["w2"], wq_sd["w2"], MI, DM, 4, 1e-8, None, 0, "q2")
    sch.barrier()
    use_big()

    def stats_rows(nt, ndim, eps, bsum, bsq, bout1, bout2, deps, mean_sub, tag):
        W0 = R_STAT
        r_m = srow.ap()[0:1, W0:W0 + nt]
        r_v = srow.ap()[0:1, W0 + 512:W0 + 512 + nt]
        r_t = srow.ap()[0:1, W0 + 1024:W0 + 1024 + nt]
        r_r2 = srow.ap()[0:1, W0 + 1536:W0 + 1536 + nt]
        out = []
        if mean_sub:
            t1 = vec(lambda e: e.tensor_scalar(r_m, PS(bsum, nt)[0:1, :],
                                               1.0 / ndim, None, ALU.mult),
                     list(deps), tag=tag + "m")
            t4a = vec(lambda e: e.tensor_scalar(r_v, PS(bsq, nt)[0:1, :],
                                                1.0 / ndim, None, ALU.mult), list(deps))
            t3 = vec(lambda e: e.tensor_mul(r_t, r_m, r_m), [t1])
            t4 = vec(lambda e: e.tensor_sub(r_v, r_v, r_t), [t4a, t3])
        else:
            t4 = vec(lambda e: e.tensor_scalar(r_v, PS(bsq, nt)[0:1, :],
                                               1.0 / ndim, None, ALU.mult), list(deps))
        t5 = vec(lambda e: e.tensor_scalar(r_v, r_v, eps, None, ALU.add), [t4])
        t6 = sca(lambda e: e.activation(out=r_t, in_=r_v, func=ACT.Ln), [t5],
                 tag=tag + "ln")
        t7 = sca(lambda e: e.activation(out=r_t, in_=r_t, func=ACT.Exp, scale=-0.5), [t6])
        t8 = vec(lambda e: e.tensor_mul(r_r2, r_t, r_t), [t7])
        t9 = vec(lambda e: e.tensor_mul(r_r2, r_v, r_r2), [t8])
        t10 = vec(lambda e: e.tensor_scalar(r_r2, r_r2, -0.5, 1.5, ALU.mult, ALU.add), [t9])
        t11 = vec(lambda e: e.tensor_mul(r_t, r_t, r_r2), [t10])
        tb1 = mmg(PS(bout1, nt), [(ONES1, r_t)], deps=[t11] + R.rdep("ones1"),
                  tag=tag + "b1")
        out.append(tb1)
        if mean_sub:
            t12 = vec(lambda e: e.tensor_mul(r_m, r_m, r_t), [t11])
            tb2 = mmg(PS(bout2, nt), [(ONES1, r_m)], deps=[t12] + R.rdep("ones1"),
                      tag=tag + "b2")
            out.append(tb2)
        return out

    def phaseAx(l):
        Lw = LI[l]
        xin = x_ext if l == 0 else x1_s
        for kk in range(KDM):
            tk = dma("sync", FAr(A_WX + kk * 2048, 2048),
                     Lw["w_in_x"][128 * kk:128 * (kk + 1), :],
                     deps=R.wdep(f"wx{kk}"), tag=f"A{l}wx")
            R.wr(tk, f"wx{kk}")
        for nm, col, w_, src in (
            ("convw", c_convw, 64, Lw["conv_w"].rearrange("(b p) j -> p b j", p=128)),
            ("convb", c_convb, 16, Lw["conv_b"].rearrange("(b p) -> p b", p=128)),
            ("dtb", c_dtb, 16, Lw["b_dt"].rearrange("(b p) -> p b", p=128)),
            ("dcol", c_dcol, 16, Lw["d"].rearrange("(b p) -> p b", p=128)),
        ):
            dstap = ccol(col, w_)
            if nm == "convw":
                dstap = dstap.rearrange("p (b j) -> p b j", j=4)
            tk = dma("sync", dstap, src, deps=R.wdep(nm), tag=nm, slow=True)
            R.wr(tk, nm)
        if l == 0:
            tk = dma("sync", ccol(c_dtw, 16),
                     Lw["w_dt"].rearrange("(b p) -> p b", p=128),
                     deps=R.wdep("dtw"), tag="dtw", slow=True)
            R.wr(tk, "dtw")
            tk = dma("sync", ccol(c_nw, 8),
                     Lw["norm_w"].rearrange("(b p) -> p b", p=128),
                     deps=R.wdep("nw"), tag="nw", slow=True)
            R.wr(tk, "nw")
        t1 = dma("sync",
                 srow.ap()[:, R_ALOG:R_ALOG + 128].rearrange("p (b n) -> p b n", n=NS),
                 Lw["a_log"][0:DH, :].rearrange("(b p) n -> p b n", p=128),
                 deps=R.wdep("rrow"), tag="alog", slow=True)
        t2 = sca(lambda e: e.activation(out=srow.ap()[:, R_ALOG + 128:R_ALOG + 256],
                                        in_=srow.ap()[:, R_ALOG:R_ALOG + 128],
                                        func=ACT.Exp), [t1], tag="aexp")
        t3 = vec(lambda e: e.tensor_scalar(ccol(c_ascale, 128),
                                           srow.ap()[:, R_ALOG + 128:R_ALOG + 256],
                                           -1.0, None, ALU.mult),
                 [t2] + R.wdep("ascale"), tag="aneg")
        R.wr(t3, "ascale")
        R.wr(t3, "rrow")
        nxp = 33 if l == 0 else 96
        if l == 1:
            for mm in range(16):
                tk = dma("sync", wdtp.ap()[:, 128 * mm:128 * (mm + 1)],
                         Lw["w_dtp"][:, 128 * mm:128 * (mm + 1)],
                         deps=R.wdep(f"wdtp{mm}"), tag="wdtp")
                R.wr(tk, f"wdtp{mm}")

        def a_tile(it, ntok, halo):
            col0 = 0 if halo else 3 + it * TT
            tl = []
            dd0 = R.wdep("xt")
            for pb in range(KDM):
                tk = dma("sync", FAr(A_XT + pb * TT, ntok),
                         xin[128 * pb:128 * (pb + 1), col0:col0 + ntok],
                         deps=dd0, tag="xld")
                tl.append(tk)
            R.wr(tl[-1], "xt")
            for tk in tl[:-1]:
                R["xt"].r.append(tk)
            def reg3(off, ntok=ntok, fah=arena["fa"]):
                if ntok == TT:
                    return fah.ap()[:, off:off + 8 * TT]
                return fah.ap()[:, off:off + 8 * TT].rearrange(
                    "p (b t) -> p b t", t=TT)[:, :, 0:ntok]
            tsq = sca(lambda e: e.activation(out=reg3(A_SQ), in_=reg3(A_XT),
                                             func=ACT.Square),
                      tl + R.wdep("sq"), tag="sq")
            R.wr(tsq, "sq")
            R.rd(tsq, "xt")
            msum = mmg(PS(4, ntok, 0, 0, 1),
                       [(ONESC, FAr(A_XT + pb * TT, ntok)) for pb in range(KDM)],
                       deps=tl + R.rdep("cons") + R.wdep("ps4"), tag="ssum")
            R.wr(msum, "ps4")
            R.rd(msum, "xt")
            msq = mmg(PS(5, ntok, 0, 0, 1),
                      [(ONESC, FAr(A_SQ + pb * TT, ntok)) for pb in range(KDM)],
                      deps=[tsq] + R.rdep("cons") + R.wdep("ps5"), tag="ssq")
            R.wr(msq, "ps5")
            R.rd(msq, "sq")
            stoks = stats_rows(ntok, DM, EPS_LN, 4, 5, 6, 7,
                               [msum, msq] + R.wdep("ps6", "ps7"), True, "ln")
            R.rd(stoks[0], "ps4", "ps5")
            R.wr(stoks[0], "ps6")
            R.wr(stoks[1], "ps7")
            txn = []
            for pb in range(KDM):
                xnp = FAr(A_XN + pb * TT, ntok)
                ta = vec(lambda e, pb=pb, xnp=xnp: e.tensor_mul(
                    xnp, FAr(A_XT + pb * TT, ntok), PS(6, ntok)),
                    stoks[:1] + tl + R.wdep("xn"), tag="xn1")
                R.rd(ta, "ps6")
                R.rd(ta, "xt")
                tb = vec(lambda e, xnp=xnp: e.tensor_sub(xnp, xnp, PS(7, ntok)),
                         [ta] + stoks[1:], tag="xn2")
                R.rd(tb, "ps7")
                txn.append(tb)
            R.wr(txn[-1], "xn")
            for tb in txn[:-1]:
                R["xn"].r.append(tb)
            if not halo:
                for pb in range(KDM):
                    tk = dma("gpsimd", xn_s[128 * pb:128 * (pb + 1), it * TT:it * TT + ntok],
                             FAr(A_XN + pb * TT, ntok), deps=[txn[pb]], tag="xnst")
                    R.rd(tk, "xn")
            if l == 0:
                tq2 = sca(lambda e: e.activation(out=reg3(A_SQ), in_=reg3(A_XN),
                                                 func=ACT.Square),
                          txn + R.wdep("sq"), tag="sq2")
                R.wr(tq2, "sq")
                R.rd(tq2, "xn")
                msq2 = mmg(PS(5, ntok, 0, 0, 1),
                           [(ONESC, FAr(A_SQ + pb * TT, ntok)) for pb in range(KDM)],
                           deps=[tq2] + R.rdep("cons") + R.wdep("ps5"), tag="ssq2")
                R.wr(msq2, "ps5")
                R.rd(msq2, "sq")
                st2 = stats_rows(ntok, DM, EPS_RMS, 4, 5, 6, 7,
                                 [msq2] + R.wdep("ps6"), False, "rms")
                R.rd(st2[0], "ps5")
                R.wr(st2[0], "ps6")
                txn2 = []
                for pb in range(KDM):
                    tk = vec(lambda e, pb=pb: e.scalar_tensor_tensor(
                        FAr(A_XT + pb * TT, ntok), FAr(A_XN + pb * TT, ntok),
                        ccol(c_nw + pb), PS(6, ntok), ALU.mult, ALU.mult),
                        st2[:1] + [txn[pb]] + R.rdep("nw") + R.wdep("xt"), tag="xn2r")
                    R.rd(tk, "ps6", "xn")
                    txn2.append(tk)
                xn2_off = A_XT
                R.wr(txn2[-1], "xt")
                for tk in txn2[:-1]:
                    R["xt"].r.append(tk)
                tx2 = txn2
                if not halo:
                    for pb in range(KDM):
                        tk = dma("gpsimd",
                                 xn2_s[128 * pb:128 * (pb + 1), it * TT:it * TT + ntok],
                                 FAr(A_XT + pb * TT, ntok), deps=[txn2[pb]], tag="x2st")
                        R.rd(tk, "xt")
            else:
                xn2_off = A_XN
                tx2 = txn
            xp_prev = None
            for pb in range(KDI):
                bank = pb % 4
                grp = mmg(PS(bank, ntok),
                          [(FAr(A_WX + kk * 2048 + pb * 128, 128),
                            FAr(xn2_off + kk * TT, ntok)) for kk in range(KDM)],
                          deps=tx2 + [R[f"wx{kk}"].w for kk in range(KDM)]
                          + R.wdep(f"ps{bank}"), tag="xmm")
                R.wr(grp, f"ps{bank}")
                if l == 0:
                    R.rd(grp, "xt")
                else:
                    R.rd(grp, "xn")
                cslot = A_CIN + (pb % 2) * 520
                if not halo:
                    tt1 = vec(lambda e, pb=pb, cslot=cslot: e.tensor_copy(
                        FAr(cslot, 3), FAr(A_TK + pb * 3, 3)),
                        R.wdep(f"cin{pb % 2}") + R.rdep(f"tk{pb}"), tag="tin")
                    R.rd(tt1, f"tk{pb}")
                else:
                    tt1 = None
                tt2 = vec(lambda e, cslot=cslot, bank=bank, ntok=ntok: e.tensor_copy(
                    FAr(cslot + 3, ntok), PS(bank, ntok)),
                    [grp, tt1] + (R.wdep(f"cin{pb % 2}") if halo else []), tag="evx")
                R.rd(tt2, f"ps{bank}")
                R.wr(tt2, f"cin{pb % 2}")
                if halo:
                    tt3 = vec(lambda e, pb=pb, cslot=cslot: e.tensor_copy(
                        FAr(A_TK + pb * 3, 3), FAr(cslot + 3, 3)),
                        [tt2] + R.wdep(f"tk{pb}"), tag="thalo")
                    R.wr(tt3, f"tk{pb}")
                    R.rd(tt3, f"cin{pb % 2}")
                    continue
                uslot = A_U + (pb % 2) * TT
                ca = vec(lambda e, pb=pb, cslot=cslot, uslot=uslot: e.tensor_scalar(
                    FAr(uslot, ntok), FAr(cslot + 0, ntok),
                    ccol(c_convw + pb * 4), ccol(c_convb + pb),
                    ALU.mult, ALU.add),
                    [tt2] + R.rdep("convw", "convb") + R.wdep(f"u{pb % 2}"), tag="cv0")
                R.rd(ca, f"cin{pb % 2}")
                for j in (1, 2, 3):
                    ca = vec(lambda e, pb=pb, j=j, cslot=cslot, uslot=uslot:
                             e.scalar_tensor_tensor(
                                 FAr(uslot, ntok), FAr(cslot + j, ntok),
                                 ccol(c_convw + pb * 4 + j),
                                 FAr(uslot, ntok), ALU.mult, ALU.add), [ca], tag="cvj")
                    R.rd(ca, f"cin{pb % 2}")
                tt4 = vec(lambda e, pb=pb, cslot=cslot, ntok=ntok: e.tensor_copy(
                    FAr(A_TK + pb * 3, 3), FAr(cslot + ntok, 3)),
                    [tt2] + R.wdep(f"tk{pb}"), tag="tout")
                R.wr(tt4, f"tk{pb}")
                R.rd(tt4, f"cin{pb % 2}")
                cu = sca(lambda e, uslot=uslot: e.activation(
                    out=FAr(uslot, ntok), in_=FAr(uslot, ntok), func=ACT.Silu),
                    [ca], tag="silu")
                R.wr(cu, f"u{pb % 2}")
                tk = dma("gpsimd", u_s[128 * pb:128 * (pb + 1), it * TT:it * TT + ntok],
                         FAr(uslot, ntok), deps=[cu], tag="ust")
                R.rd(tk, f"u{pb % 2}")
                wxt = FAr(A_ZE + (pb % 2) * 512, nxp)
                tk2 = dma("sync", wxt, Lw["w_xproj"][128 * pb:128 * (pb + 1), :],
                          deps=R.wdep(f"ze{pb % 2}"), tag="wxp")
                R.wr(tk2, f"ze{pb % 2}")
                mm1 = ten(lambda e, pb=pb, wxt=wxt, ntok=ntok, nxp=nxp: e.matmul(
                    PS(6, ntok, 0, 0, nxp), wxt, FAr(A_U + (pb % 2) * TT, ntok),
                    start=(pb == 0), stop=(pb == KDI - 1)),
                    [cu, tk2] + ([xp_prev] if xp_prev else R.wdep("ps6")), tag="xpmm")
                R.rd(mm1, f"u{pb % 2}", f"ze{pb % 2}")
                xp_prev = mm1
            if halo:
                return
            R.wr(xp_prev, "ps6")
            te = vec(lambda e, ntok=ntok, nxp=nxp: e.tensor_copy(
                FAr(A_XD, ntok, 0, nxp), PS(6, ntok, 0, 0, nxp)),
                [xp_prev] + R.wdep("xd"), tag="xdev")
            R.rd(te, "ps6")
            R.wr(te, "xd")
            brow = 1 if l == 0 else 64
            tk = dma("gpsimd", cc[f"bc_in{l}"][0:32, it * TT:it * TT + ntok],
                     FAr(A_XD, ntok, brow, brow + 32), deps=[te], tag="bcst")
            R.rd(tk, "xd")
            if l == 0:
                tbd = mmg(PS(4, ntok), [(ONES1, FAr(A_XD, ntok, 0, 1))],
                          deps=[te] + R.rdep("ones1") + R.wdep("ps4"), tag="dtb")
                R.wr(tbd, "ps4")
                R.rd(tbd, "xd")
            for pb in range(KDI):
                pre = FAr(A_DT, ntok)
                t2a = FAr(A_DT + 512, ntok)
                if l == 0:
                    tp = vec(lambda e, pb=pb: e.tensor_scalar(
                        pre, PS(4, ntok), ccol(c_dtw + pb), ccol(c_dtb + pb),
                        ALU.mult, ALU.add),
                        R.rdep("ps4") + R.rdep("dtw", "dtb") + R.wdep("dt0"), tag="pre")
                    R.rd(tp, "ps4")
                else:
                    bank = pb % 4
                    mmq = ten(lambda e, pb=pb, bank=bank, ntok=ntok: e.matmul(
                        PS(bank, ntok), wdtp.ap()[:, 128 * pb:128 * (pb + 1)],
                        FAr(A_XD, ntok, 0, 64), start=True, stop=True),
                        [te, R[f"wdtp{pb}"].w] + R.wdep(f"ps{bank}"), tag="dtmm")
                    R.rd(mmq, "xd")
                    R.wr(mmq, f"ps{bank}")
                    tp = vec(lambda e, pb=pb, bank=bank: e.tensor_scalar(
                        pre, PS(bank, ntok), ccol(c_dtb + pb), None, ALU.add),
                        [mmq] + R.rdep("dtb") + R.wdep("dt0"), tag="pre1")
                    R.rd(tp, f"ps{bank}")
                R.wr(tp, "dt0")
                ta = sca(lambda e: e.activation(out=t2a, in_=pre, func=ACT.Abs),
                         [tp] + R.wdep("dt1"), tag="dabs")
                tb2 = sca(lambda e: e.activation(out=t2a, in_=t2a, func=ACT.Exp,
                                                 scale=-1.0), [ta])
                tc2 = sca(lambda e: e.activation(out=t2a, in_=t2a, func=ACT.Ln,
                                                 bias=ONESC), [tb2])
                R.rd(tc2, "cons")
                R.wr(tc2, "dt1")
                td = vec(lambda e: e.scalar_tensor_tensor(
                    pre, pre, 0.0, t2a, ALU.max, ALU.add), [tc2, tp], tag="delta")
                R.wr(td, "dt0")
                R.rd(td, "dt1")
                if pb < NDB:
                    tk = dma("gpsimd", m_loc[128 * pb:128 * (pb + 1), it * TT:it * TT + ntok],
                             pre, deps=[td], tag="mst")
                else:
                    tk = dma("gpsimd",
                             cc[f"m_in{l}"][128 * (pb - NDB):128 * (pb - NDB + 1),
                                            it * TT:it * TT + ntok],
                             pre, deps=[td], tag="mstc")
                R.rd(tk, "dt0")
                ure = FAr(A_DT + 1024, ntok)
                tk2 = dma("sync", ure, u_s[128 * pb:128 * (pb + 1), it * TT:it * TT + ntok],
                          deps=[td] + R.wdep("dt2"), tag="url")
                R.wr(tk2, "dt2")
                tw = vec(lambda e, o=FAr(A_WW, ntok): e.tensor_mul(o, pre, ure),
                         [td, tk2] + R.wdep("ww"), tag="wmul")
                R.rd(tw, "dt0", "dt2")
                R.wr(tw, "ww")
                if pb < NDB:
                    tk3 = dma("gpsimd", w_loc[128 * pb:128 * (pb + 1), it * TT:it * TT + ntok],
                              FAr(A_WW, ntok), deps=[tw], tag="wst")
                else:
                    tk3 = dma("gpsimd",
                              cc[f"w_in{l}"][128 * (pb - NDB):128 * (pb - NDB + 1),
                                             it * TT:it * TT + ntok],
                              FAr(A_WW, ntok), deps=[tw], tag="wstc")
                R.rd(tk3, "ww")

        a_tile(0, 3, True)
        for it in range(NTILE):
            a_tile(it, TT, False)
        sch.barrier()

    def phaseAz(l):
        Lw = LI[l]
        src2 = xn2_s if l == 0 else xn_s
        for kk in range(KDM):
            tk = dma("sync", FAr(A_WX + kk * 2048, 2048),
                     Lw["w_in_z"][128 * kk:128 * (kk + 1), :],
                     deps=R.wdep(f"wx{kk}"), tag=f"A{l}wz")
            R.wr(tk, f"wx{kk}")
        for it in range(NTILE):
            tl = []
            dd0 = R.wdep("xn")
            for kk in range(KDM):
                tk = dma("sync", FAr(A_XN + kk * TT, TT),
                         src2[128 * kk:128 * (kk + 1), it * TT:(it + 1) * TT],
                         deps=dd0, tag="zxl")
                tl.append(tk)
            R.wr(tl[-1], "xn")
            for tk in tl[:-1]:
                R["xn"].r.append(tk)
            for pb in range(KDI):
                bank = pb % 4
                grp = mmg(PS(bank, TT),
                          [(FAr(A_WX + kk * 2048 + pb * 128, 128),
                            FAr(A_XN + kk * TT, TT)) for kk in range(KDM)],
                          deps=tl + [R[f"wx{kk}"].w for kk in range(KDM)]
                          + R.wdep(f"ps{bank}"), tag="zmm")
                R.wr(grp, f"ps{bank}")
                R.rd(grp, "xn")
                ze = FAr(A_ZE + (pb % 2) * 512, TT)
                tv2 = vec(lambda e, bank=bank, ze=ze: e.tensor_copy(ze, PS(bank, TT)),
                          [grp] + R.wdep(f"ze{pb % 2}"), tag="zev")
                R.rd(tv2, f"ps{bank}")
                R.wr(tv2, f"ze{pb % 2}")
                tk = dma("gpsimd", z_s[128 * pb:128 * (pb + 1), it * TT:(it + 1) * TT],
                         ze, deps=[tv2], tag="zst")
                R.rd(tk, f"ze{pb % 2}")
        sch.barrier()

    def row_ops_scale(row_off, kcell_col, krow, deps, tag):
        rmax = srow.ap()[0:1, row_off:row_off + 512]
        rs = srow.ap()[0:1, row_off + 512:row_off + 1024]
        rb = srow.ap()[0:1, row_off + 1024:row_off + 1536]
        t1 = vec(lambda e: e.tensor_scalar(rmax, rmax, 1e-5, None, ALU.max),
                 list(deps) + R.wdep("brows"), tag=tag + "cl")
        t2 = vec(lambda e: e.reciprocal(rs, rmax), [t1])
        t3 = vec(lambda e: e.tensor_scalar(rs, rs, 127.0, None, ALU.mult), [t2])
        t4 = vec(lambda e: e.tensor_scalar(rb, rmax, 1.0 / 127.0, None, ALU.mult), [t1])
        t5 = vec(lambda e: e.tensor_scalar(
            rb, rb, cons.ap()[0:1, c_kap + krow:c_kap + krow + 1], None,
            ALU.mult), [t4], tag=tag + "kb")
        R.wr(t5, "brows")
        return t3, t5

    def phaseB0():
        for kk in range(KDM):
            tk = dma("sync", WQv(kk * MI, MI), wq_sd["g"][128 * kk:128 * (kk + 1), :],
                     deps=R.wdep(f"bw{kk}"), tag="Bwg")
            R.wr(tk, f"bw{kk}")
        for it in range(NTILE):
            tl = []
            dd0 = R.wdep("bxn")
            for pb in range(KDM):
                tk = dma("sync", FAr(B_XN + pb * TT, TT),
                         xn_s[128 * pb:128 * (pb + 1), it * TT:(it + 1) * TT],
                         deps=dd0, tag="bxl")
                tl.append(tk)
            R.wr(tl[-1], "bxn")
            for tk in tl[:-1]:
                R["bxn"].r.append(tk)
            acc = FAr(B_T0, TT)
            tab = sca(lambda e, o=FAr(B_ABS, 8 * TT): e.activation(out=o,
                                             in_=FAr(B_XN, 8 * TT), func=ACT.Abs),
                      tl + R.wdep("babs"), tag="babs")
            R.rd(tab, "bxn")
            R.wr(tab, "babs")
            tprev = vec(lambda e, a=FAr(B_ABS, TT), b=FAr(B_ABS + TT, TT): e.tensor_max(acc, a, b),
                        [tab] + R.wdep("bt0"), tag="amx")
            for pb in range(2, KDM):
                tprev = vec(lambda e, pb=pb: e.tensor_max(
                    acc, acc, FAr(B_ABS + pb * TT, TT)), [tprev], tag="amx")
            R.rd(tprev, "babs")
            R.wr(tprev, "bt0")
            tpm = pmax_row(acc, R_BR, [tprev], tag="pmx")
            R.rd(tpm, "bt0")
            ts, tb = row_ops_scale(R_BR, c_kap, 0, [tpm], "sg")
            tkst = dma("gpsimd", srow_s[0:1, it * TT:(it + 1) * TT],
                       srow.ap()[0:1, R_BR:R_BR + 512], deps=[ts, tb],
                       tag="sst")
            R.rd(tkst, "brows")
            tb6 = mmg(PS(6, TT), [(ONES1, srow.ap()[0:1, R_BR + 512:R_BR + 1024])],
                      deps=[ts] + R.rdep("ones1") + R.wdep("ps6"), tag="sbc")
            R.wr(tb6, "ps6")
            R.rd(tb6, "brows")
            tb5m = mmg(PS(5, TT), [(ONES1, srow.ap()[0:1, R_BR + 1024:R_BR + 1536])],
                      deps=[tb] + R.rdep("ones1") + R.wdep("ps5"), tag="gbc")
            R.wr(tb5m, "ps5")
            R.rd(tb5m, "brows")
            tb5 = vec(lambda e, o=FAr(B_SB5, TT): e.tensor_copy(o, PS(5, TT)),
                      [tb5m] + R.wdep("bsb5"), tag="gbce")
            R.rd(tb5, "ps5")
            R.wr(tb5, "bsb5")
            tq = []
            for pb in range(KDM):
                v = FAr(B_T0 + 512, TT)
                tv = vec(lambda e, pb=pb, v=v: e.tensor_mul(
                    v, FAr(B_XN + pb * TT, TT), PS(6, TT)),
                    [tb6, tl[pb]] + R.wdep("bt1"), tag="qv")
                R.rd(tv, "ps6", "bxn")
                R.wr(tv, "bt1")
                tr = round_p1(v, BAr(pb * TT, TT),
                              [tv] + (R.wdep("bq") if pb == 0 else []), -127.0, 128.0)
                R.rd(tr, "bt1")
                tq.append(tr)
                tk = dma("gpsimd", xq_s[128 * pb:128 * (pb + 1), it * TT:(it + 1) * TT],
                         BAr(pb * TT, TT), deps=[tr], tag="xqst")
                tq.append(tk)
            R.wr(tq[-2], "bq")
            for tk in tq:
                R.rd(tk, "bq") if tk[2] else None
            R.rd(tq[-1], "bq")
            for mm in range(KMI):
                bank = mm % 4
                grp = mmg(PS(bank, TT),
                          [(WQv(kk * MI + mm * 128, 128), BAr(kk * TT, TT))
                           for kk in range(KDM)],
                          deps=tq + [R[f"bw{kk}"].w for kk in range(KDM)]
                          + R.wdep(f"ps{bank}"), tag="gmm")
                R.wr(grp, f"ps{bank}")
                R.rd(grp, "bq")
                go = FAr(B_T0 + 1024, TT)
                tg = vec(lambda e, bank=bank, mm=mm, go=go: e.scalar_tensor_tensor(
                    go, PS(bank, TT), ccol(c_csg + mm), FAr(B_SB5, TT),
                    ALU.subtract, ALU.mult),
                    [grp, tb5] + R.rdep("cons2") + R.wdep("bt2"), tag="gep")
                R.rd(tg, f"ps{bank}", "bsb5")
                R.wr(tg, "bt2")
                tsg = sca(lambda e, go=go: e.activation(out=go, in_=go, func=ACT.Silu),
                          [tg], tag="gsl")
                R.wr(tsg, "bt2")
                tk = dma("gpsimd", sg_s[128 * mm:128 * (mm + 1), it * TT:(it + 1) * TT],
                         go, deps=[tsg], tag="sgst")
                R.rd(tk, "bt2")
        sch.barrier()
        for kk in range(KDM):
            tk = dma("sync", WQv(kk * MI, MI), wq_sd["u"][128 * kk:128 * (kk + 1), :],
                     deps=R.wdep(f"bw{kk}"), tag="Bwu")
            R.wr(tk, f"bw{kk}")
        for it in range(NTILE):
            tq = []
            dd0 = R.wdep("bq")
            for pb in range(KDM):
                tk = dma("sync", BAr(pb * TT, TT),
                         xq_s[128 * pb:128 * (pb + 1), it * TT:(it + 1) * TT],
                         deps=dd0, tag="xql")
                tq.append(tk)
            R.wr(tq[-1], "bq")
            tms = dma("sync", srow.ap()[0:1, R_BR:R_BR + 512],
                      srow_s[0:1, it * TT:(it + 1) * TT],
                      deps=R.wdep("pm") + R.wdep("brows"), tag="sld")
            R.wr(tms, "pm")
            R.wr(tms, "brows")
            ts, tb = row_ops_scale(R_BR, c_kap, 1, [tms], "su")
            tb5m = mmg(PS(5, TT),
                      [(ONES1, srow.ap()[0:1, R_BR + 1024:R_BR + 1536])],
                      deps=[tb] + R.rdep("ones1") + R.wdep("ps5"), tag="ubc")
            R.wr(tb5m, "ps5")
            R.rd(tb5m, "brows")
            tb5 = vec(lambda e, o=FAr(B_SB5, TT): e.tensor_copy(o, PS(5, TT)),
                      [tb5m] + R.wdep("bsb5"), tag="ubce")
            R.rd(tb5, "ps5")
            R.wr(tb5, "bsb5")
            gacc = FAr(B_T0 + 2048, TT)
            tga = vec(lambda e: e.memset(gacc, 0.0), R.wdep("bt3"), tag="gz")
            R.wr(tga, "bt3")
            for mm in range(KMI):
                bank = mm % 4
                grp = mmg(PS(bank, TT),
                          [(WQv(kk * MI + mm * 128, 128), BAr(kk * TT, TT))
                           for kk in range(KDM)],
                          deps=tq + [R[f"bw{kk}"].w for kk in range(KDM)]
                          + R.wdep(f"ps{bank}"), tag="umm")
                R.wr(grp, f"ps{bank}")
                R.rd(grp, "bq")
                uo = FAr(B_T0, TT)
                tu = vec(lambda e, bank=bank, mm=mm, uo=uo: e.scalar_tensor_tensor(
                    uo, PS(bank, TT), ccol(c_csu + mm), FAr(B_SB5, TT),
                    ALU.subtract, ALU.mult),
                    [grp, tb5] + R.rdep("cons2") + R.wdep("bt0"), tag="uep")
                R.rd(tu, f"ps{bank}", "bsb5")
                R.wr(tu, "bt0")
                sgt = FAr(B_T0 + 512, TT)
                tk = dma("sync", sgt, sg_s[128 * mm:128 * (mm + 1), it * TT:(it + 1) * TT],
                         deps=R.wdep("bt1"), tag="sgl")
                R.wr(tk, "bt1")
                tgu = vec(lambda e, uo=uo, sgt=sgt: e.tensor_mul(uo, uo, sgt),
                          [tu, tk], tag="gumul")
                R.rd(tgu, "bt1")
                R.wr(tgu, "bt0")
                tk2 = dma("gpsimd", gu_s[128 * mm:128 * (mm + 1), it * TT:(it + 1) * TT],
                          uo, deps=[tgu], tag="gust")
                R.rd(tk2, "bt0")
                tab2 = sca(lambda e, uo=uo: e.activation(
                    out=FAr(B_ABS, TT), in_=uo, func=ACT.Abs),
                    [tgu] + R.wdep("babs"), tag="gabs")
                R.rd(tab2, "bt0")
                R.wr(tab2, "babs")
                tga = vec(lambda e, b=FAr(B_ABS, TT): e.tensor_max(gacc, gacc, b),
                          [tab2, tga], tag="gumx")
                R.rd(tga, "babs")
            R.wr(tga, "bt3")
            tpm = pmax_row(gacc, R_BR, [tga], tag="pmg")
            R.rd(tpm, "bt3")
            ts2, tb2 = row_ops_scale(R_BR, c_kap, 2, [tpm], "sd")
            tk = dma("gpsimd", srow_s[1:2, it * TT:(it + 1) * TT],
                     srow.ap()[0:1, R_BR + 512:R_BR + 1024],
                     deps=[ts2], tag="sg2st")
            R.rd(tk, "brows")
            tk = dma("gpsimd", srow_s[2:3, it * TT:(it + 1) * TT],
                     srow.ap()[0:1, R_BR + 1024:R_BR + 1536],
                     deps=[tb2], tag="sg3st")
            R.rd(tk, "pm")
            R.rd(tk, "brows")
        sch.barrier()
        for kk in range(KMI):
            tk = dma("sync", WQv(kk * 512, 512),
                     wq_sd["d"][128 * kk:128 * (kk + 1), 0:512],
                     deps=R.wdep(f"bw{kk}"), tag="Bwd")
            R.wr(tk, f"bw{kk}")
            tk = dma("sync", WQv(kk * 512 + 16384, 512),
                     wq_sd["d"][128 * kk:128 * (kk + 1), 512:1024],
                     deps=R.wdep(f"bw2{kk}"), tag="Bwd2")
            R.wr(tk, f"bw2{kk}")
        for it in range(NTILE):
            tms = dma("sync", srow.ap()[0:1, R_BR:R_BR + 512],
                      srow_s[1:2, it * TT:(it + 1) * TT],
                      deps=R.wdep("pm") + R.wdep("brows"), tag="sld2")
            R.wr(tms, "pm")
            R.wr(tms, "brows")
            tmb = dma("sync", srow.ap()[0:1, R_BR + 1024:R_BR + 1536],
                      srow_s[2:3, it * TT:(it + 1) * TT], deps=[tms], tag="sld3")
            R.wr(tmb, "brows2")
            tb6 = mmg(PS(6, TT), [(ONES1, srow.ap()[0:1, R_BR:R_BR + 512])],
                      deps=[tms] + R.rdep("ones1") + R.wdep("ps6"), tag="dsbc")
            R.wr(tb6, "ps6")
            R.rd(tb6, "brows")
            tb5m = mmg(PS(5, TT), [(ONES1, srow.ap()[0:1, R_BR + 1024:R_BR + 1536])],
                      deps=[tmb] + R.rdep("ones1") + R.wdep("ps5"), tag="dbbc")
            R.wr(tb5m, "ps5")
            R.rd(tb5m, "brows2")
            tb5 = vec(lambda e, o=FAr(B_SB5, TT): e.tensor_copy(o, PS(5, TT)),
                      [tb5m] + R.wdep("bsb5"), tag="dbbe")
            R.rd(tb5, "ps5")
            R.wr(tb5, "bsb5")
            tqs = []
            for kk in range(KMI):
                gt = FAr(B_T0 + (kk % 2) * 512, TT)
                tk = dma("sync", gt, gu_s[128 * kk:128 * (kk + 1), it * TT:(it + 1) * TT],
                         deps=R.wdep(f"bt{kk % 2}"), tag="gul")
                R.wr(tk, f"bt{kk % 2}")
                tv = vec(lambda e, gt=gt: e.tensor_mul(gt, gt, PS(6, TT)), [tk, tb6],
                         tag="gqv")
                R.rd(tv, "ps6")
                R.wr(tv, f"bt{kk % 2}")
                tr = round_p1(gt, BAr(kk * TT, TT),
                              [tv] + (R.wdep("bq") if kk == 0 else []), -127.0, 128.0)
                R.rd(tr, f"bt{kk % 2}")
                tqs.append(tr)
            R.wr(tqs[-1], "bq")
            for mm in range(KDM):
                bank = mm % 4
                pairs = [(WQv(kk * 512 + (0 if mm < 4 else 16384) + (mm % 4) * 128, 128),
                          BAr(kk * TT, TT)) for kk in range(KMI)]
                grp = mmg(PS(bank, TT), pairs,
                          deps=tqs + [R[f"bw{kk}"].w for kk in range(KMI)]
                          + [R[f"bw2{kk}"].w for kk in range(KMI)]
                          + R.wdep(f"ps{bank}"), tag="dmm")
                R.wr(grp, f"ps{bank}")
                R.rd(grp, "bq")
                go = FAr(B_T0 + 1024, TT)
                tg = vec(lambda e, bank=bank, mm=mm, go=go: e.scalar_tensor_tensor(
                    go, PS(bank, TT), ccol(c_csd + mm), FAr(B_SB5, TT),
                    ALU.subtract, ALU.mult),
                    [grp, tb5] + R.rdep("cons2") + R.wdep("bt2"), tag="dep")
                R.rd(tg, f"ps{bank}", "bsb5")
                R.wr(tg, "bt2")
                tsg = sca(lambda e, go=go: e.activation(out=go, in_=go, func=ACT.Silu),
                          [tg], tag="dsl")
                R.wr(tsg, "bt2")
                tk = dma("gpsimd", g_s[128 * mm:128 * (mm + 1), it * TT:(it + 1) * TT],
                         go, deps=[tsg], tag="gst")
                R.rd(tk, "bt2")
        sch.barrier()

    def phaseB1():
        tkb = dma("sync", ccol(c_b1, 32), LI[1]["b1"].rearrange("(b p) -> p b", p=128),
                  deps=R.wdep("b1"), tag="b1l", slow=True)
        R.wr(tkb, "b1")
        tkb2 = dma("sync", ccol(c_b2, 8), LI[1]["b2"].rearrange("(b p) -> p b", p=128),
                   deps=R.wdep("b2"), tag="b2l", slow=True)
        R.wr(tkb2, "b2")
        for kk in range(KDM):
            tk = dma("sync", WQv(kk * MI, MI), wq_sd["w1"][128 * kk:128 * (kk + 1), :],
                     deps=R.wdep(f"bw{kk}"), tag="Bw1")
            R.wr(tk, f"bw{kk}")
        for it in range(NTILE):
            tl = []
            dd0 = R.wdep("bxn")
            for pb in range(KDM):
                tk = dma("sync", FAr(B_XN + pb * TT, TT),
                         xn_s[128 * pb:128 * (pb + 1), it * TT:(it + 1) * TT],
                         deps=dd0, tag="bxl1")
                tl.append(tk)
            R.wr(tl[-1], "bxn")
            for tk in tl[:-1]:
                R["bxn"].r.append(tk)
            tq = []
            for pb in range(KDM):
                th = sca(lambda e, pb=pb: e.activation(
                    out=BAr(pb * TT, TT), in_=FAr(B_XN + pb * TT, TT), func=ACT.Copy),
                    [tl[pb]] + R.wdep("bq"), tag="hic")
                tlo = vec(lambda e, pb=pb: e.tensor_sub(
                    BAr(4096 + pb * TT, TT), FAr(B_XN + pb * TT, TT), BAr(pb * TT, TT)),
                    [th], tag="loc")
                R.rd(tlo, "bxn")
                tq.append(tlo)
            R.wr(tq[-1], "bq")
            for mm in range(KMI):
                bank = mm % 4
                pairs = ([(WQv(kk * MI + mm * 128, 128), BAr(kk * TT, TT))
                          for kk in range(KDM)]
                         + [(WQv(kk * MI + mm * 128, 128), BAr(4096 + kk * TT, TT))
                            for kk in range(KDM)])
                grp = mmg(PS(bank, TT), pairs,
                          deps=tq + [R[f"bw{kk}"].w for kk in range(KDM)]
                          + R.wdep(f"ps{bank}"), tag="w1mm")
                R.wr(grp, f"ps{bank}")
                R.rd(grp, "bq")
                h1 = FAr(B_T0, TT)
                tg = sca(lambda e, bank=bank, mm=mm, h1=h1: e.activation(
                    out=h1, in_=PS(bank, TT), func=ACT.Gelu, bias=ccol(c_b1 + mm)),
                    [grp] + R.rdep("b1") + R.wdep("bt0"), tag="gelu")
                R.rd(tg, f"ps{bank}")
                R.wr(tg, "bt0")
                th2 = sca(lambda e, h1=h1: e.activation(
                    out=BAr(8192, TT), in_=h1, func=ACT.Copy),
                    [tg] + R.wdep("bhl"), tag="h1h")
                tl2 = vec(lambda e, h1=h1: e.tensor_sub(BAr(8192 + TT, TT), h1,
                                                        BAr(8192, TT)), [th2], tag="h1l")
                R.rd(tl2, "bt0")
                R.wr(tl2, "bhl")
                tk = dma("gpsimd", h1hi_s[128 * mm:128 * (mm + 1), it * TT:(it + 1) * TT],
                         BAr(8192, TT), deps=[tl2], tag="h1hst")
                tk2 = dma("gpsimd", h1lo_s[128 * mm:128 * (mm + 1), it * TT:(it + 1) * TT],
                          BAr(8192 + TT, TT), deps=[tl2], tag="h1lst")
                R.rd(tk, "bhl")
                R.rd(tk2, "bhl")
        sch.barrier()
        for kk in range(KMI):
            tk = dma("sync", WQv(kk * 512, 512),
                     wq_sd["w2"][128 * kk:128 * (kk + 1), 0:512],
                     deps=R.wdep(f"bw{kk}"), tag="Bw2a")
            R.wr(tk, f"bw{kk}")
            tk = dma("sync", WQv(kk * 512 + 16384, 512),
                     wq_sd["w2"][128 * kk:128 * (kk + 1), 512:1024],
                     deps=R.wdep(f"bw2{kk}"), tag="Bw2b")
            R.wr(tk, f"bw2{kk}")
        for it in range(NTILE):
            open_grp = {}
            for half in range(2):
                tq = []
                dd0 = R.wdep("bq")
                for kx in range(16):
                    kk = half * 16 + kx
                    t1 = dma("sync", BAr(kx * TT, TT),
                             h1hi_s[128 * kk:128 * (kk + 1), it * TT:(it + 1) * TT],
                             deps=dd0, tag="h1hl")
                    t2 = dma("sync", BAr(8192 + kx * TT, TT),
                             h1lo_s[128 * kk:128 * (kk + 1), it * TT:(it + 1) * TT],
                             deps=dd0, tag="h1ll")
                    tq += [t1, t2]
                R.wr(tq[-1], "bq")
                for tk in tq[:-1]:
                    R["bq"].r.append(tk)
                for mm in range(KDM):
                    bank = mm
                    pairs = []
                    for kx in range(16):
                        kk = half * 16 + kx
                        wv = WQv(kk * 512 + (0 if mm < 4 else 16384) + (mm % 4) * 128, 128)
                        pairs.append((wv, BAr(kx * TT, TT)))
                        pairs.append((wv, BAr(8192 + kx * TT, TT)))

                    def emit(eng, pairs=tuple(pairs), bank=bank, half=half):
                        ins = None
                        n = len(pairs)
                        for i, (l2, r2) in enumerate(pairs):
                            ins = eng.matmul(PS(bank, TT), l2, r2,
                                             start=(half == 0 and i == 0),
                                             stop=(half == 1 and i == n - 1))
                        return ins
                    deps = tq + [R[f"bw{kk}"].w for kk in range(KMI)] \
                        + [R[f"bw2{kk}"].w for kk in range(KMI)]
                    if half == 0:
                        deps = deps + R.wdep(f"ps{bank}")
                    else:
                        deps = deps + [open_grp[bank]]
                    g2 = sch.op("tensor", emit, deps, tag="w2mm")
                    open_grp[bank] = g2
                    if half == 1:
                        R.wr(g2, f"ps{bank}")
                        R.rd(g2, "bq")
                    else:
                        R.rd(g2, "bq")
            for mm in range(KDM):
                go = FAr(B_T0 + (mm % 2) * 512, TT)
                tg = sca(lambda e, mm=mm, go=go: e.activation(
                    out=go, in_=PS(mm, TT), func=ACT.Silu, bias=ccol(c_b2 + mm)),
                    [open_grp[mm]] + R.rdep("b2") + R.wdep(f"bt{mm % 2}"), tag="w2ep")
                R.rd(tg, f"ps{mm}")
                R.wr(tg, f"bt{mm % 2}")
                tk = dma("gpsimd", g_s[128 * mm:128 * (mm + 1), it * TT:(it + 1) * TT],
                         go, deps=[tg], tag="gst1")
                R.rd(tk, f"bt{mm % 2}")
        sch.barrier()

    def phaseC(l):
        tz = vec(lambda e: e.memset(carry.ap()[:, :], 0.0), R.wdep("carry"), tag="cz")
        R.wr(tz, "carry")
        tsel = dma("sync", FAr(C_SEL, 32 * 128, 0, 32), seln_in[:, :],
                   deps=R.wdep("csel"), tag="sel")
        R.wr(tsel, "csel")
        agy = []
        for c in range(NCHUNK):
            lc = (c % 4) * TC
            cb0 = FAr(C_A, TC, 0, 32)
            cb1 = FAr(C_A + TC, TC, 0, 32)
            dst_bc = FAr(C_BC, TC, 0, 32)
            ta = dma("sync", dst_bc, cc[f"bc_in{l}"][0:32, lc:lc + TC],
                     deps=R.wdep("cbc"), tag="bcl0")
            tb = dma("sync", cb0, cc[f"bc_out{l}"][0:32, lc:lc + TC],
                     deps=R.wdep("cA0") + [R[f"ag_bc{l}"].w], tag="bcl1")
            tc3 = dma("sync", cb1, cc[f"bc_out{l}"][32:64, lc:lc + TC],
                      deps=R.wdep("cA1") + [tb], tag="bcl2")
            R.wr(tb, "cA0")
            R.wr(tc3, "cA1")
            t1 = vec(lambda e, m=mcol(3 * c)[0:32, :], dst=dst_bc: e.tensor_scalar(
                dst, dst, m, None, ALU.mult), [ta] + R.rdep("msk"), tag="bcc1")
            t2 = vec(lambda e, m=mcol(3 * c + 1)[0:32, :], dst=dst_bc, s=cb0:
                     e.scalar_tensor_tensor(dst, s, m, dst, ALU.mult, ALU.add),
                     [t1, tb], tag="bcc2")
            t3 = vec(lambda e, m=mcol(3 * c + 2)[0:32, :], dst=dst_bc, s=cb1:
                     e.scalar_tensor_tensor(dst, s, m, dst, ALU.mult, ALU.add),
                     [t2, tc3], tag="bcc3")
            R.wr(t3, "cbc")
            R.rd(t2, "cA0")
            R.rd(t3, "cA1")
            tmw = {}
            for j, (loc, s0t) in enumerate(((m_loc, "m"), (w_loc, "w"))):
                o0 = cc[f"{s0t}_out{l}"]
                for g in range(2):
                    dst = FAr(C_MW + j * 8192 + g * 4096, 4096)
                    dl0, dl1, dl2 = [], [], []
                    for q in range(C_G):
                        db = g * C_G + q
                        dl0.append(dma("sync",
                                       FAr(C_MW + j * 8192 + db * TC, TC),
                                       loc[128 * db:128 * (db + 1), lc:lc + TC],
                                       deps=R.wdep(f"mw{j}{g}") if q == 0 else [],
                                       tag="mwl0"))
                        dl1.append(dma("sync", FAr(C_A + q * TC, TC),
                                       o0[128 * db:128 * (db + 1), lc:lc + TC],
                                       deps=R.wdep(f"cA{q}")
                                       + [R[f"ag_{s0t}{l}"].w], tag="mwl1"))
                        dl2.append(dma("sync", FAr(C_B + q * TC, TC),
                                       o0[DH + 128 * db:DH + 128 * (db + 1),
                                          lc:lc + TC],
                                       deps=(R.wdep("cB") if q == 0 else [])
                                       + [R[f"ag_{s0t}{l}"].w], tag="mwl2"))
                    for q in range(C_G):
                        R.wr(dl1[q], f"cA{q}")
                    R.wr(dl2[-1], "cB")
                    u1 = vec(lambda e, dst=dst, m=mcol(3 * c): e.tensor_scalar(
                        dst, dst, m, None, ALU.mult), dl0 + R.rdep("msk"),
                        tag="mwc1")
                    u2 = vec(lambda e, dst=dst, m=mcol(3 * c + 1),
                             s=FAr(C_A, C_G * TC): e.scalar_tensor_tensor(
                        dst, s, m, dst, ALU.mult, ALU.add), [u1] + dl1,
                        tag="mwc2")
                    for q in range(C_G):
                        R.rd(u2, f"cA{q}")
                    u3 = vec(lambda e, dst=dst, m=mcol(3 * c + 2),
                             s=FAr(C_B, C_G * TC): e.scalar_tensor_tensor(
                        dst, s, m, dst, ALU.mult, ALU.add), [u2] + dl2,
                        tag="mwc3")
                    R.wr(u3, f"mw{j}{g}")
                    R.rd(u3, "cB")
                    tmw[(j, g)] = u3
            for n in range(NS):
                par = n % 2
                bB = par * 2
                bC = 4 + par * 2
                tbs, tcs2 = [], []
                for qq in range(2):
                    tbb = mmg(PS(bB + qq, TT),
                              [(FAr(C_SEL + n * 128, 128, 0, 32),
                                FAr(C_BC + qq * TT, TT, 0, 32))],
                              deps=[t3] + R.rdep("csel")
                              + R.wdep(f"cps{bB + qq}"), tag="Bbc")
                    R.wr(tbb, f"cps{bB + qq}")
                    R.rd(tbb, "cbc")
                    tbs.append(tbb)
                for qq in range(2):
                    tcc2 = mmg(PS(bC + qq, TT),
                               [(FAr(C_SEL + (16 + n) * 128, 128, 0, 32),
                                 FAr(C_BC + qq * TT, TT, 0, 32))],
                               deps=[t3] + R.rdep("csel")
                               + R.wdep(f"cps{bC + qq}"), tag="Cbc")
                    R.wr(tcc2, f"cps{bC + qq}")
                    R.rd(tcc2, "cbc")
                    tcs2.append(tcc2)
                B_rep = ps8.ap()[:, bB * 512:bB * 512 + TC].unsqueeze(
                    1).broadcast_to([128, C_G, TC])
                C_rep = ps8.ap()[:, bC * 512:bC * 512 + TC].unsqueeze(
                    1).broadcast_to([128, C_G, TC])
                for g in range(2):
                    atoks = []
                    for q in range(C_G):
                        db = g * C_G + q
                        col = db * 16 + n
                        atoks.append(sca(lambda e, q=q, db=db, col=col:
                                         e.activation(
                            out=FAr(C_A + q * TC, TC),
                            in_=FAr(C_MW + db * TC, TC), func=ACT.Exp,
                            scale=ccol(c_ascale + col)),
                            R.wdep(f"cA{q}") + [tmw[(0, g)]]
                            + R.rdep("ascale"), tag="aexp"))
                    for q, t_ in enumerate(atoks):
                        R.rd(t_, f"mw0{g}")
                        R.wr(t_, f"cA{q}")
                    tb2 = vec(lambda e, g=g, br=B_rep: e.tensor_tensor(
                        FAr(C_B, C_G * TC).rearrange("p (d t) -> p d t", t=TC),
                        FAr(C_MW + 8192 + g * C_G * TC, C_G * TC).rearrange(
                            "p (d t) -> p d t", t=TC),
                        br, ALU.mult),
                        R.wdep("cB") + [tmw[(1, g)]] + tbs, tag="bmul")
                    R.wr(tb2, "cB")
                    R.rd(tb2, f"mw1{g}", f"cps{bB}", f"cps{bB + 1}")
                    swar = R.wdep("cH")
                    stoks = []
                    for q in range(C_G):
                        db = g * C_G + q
                        col = db * 16 + n
                        init = carry.ap()[:, col:col + 1] if c > 0 else 0.0
                        stoks.append(vec(lambda e, q=q, init=init:
                                         e.tensor_tensor_scan(
                            FAr(C_H + q * TC, TC), FAr(C_A + q * TC, TC),
                            FAr(C_B + q * TC, TC), init, ALU.mult, ALU.add),
                            swar + [atoks[q], tb2]
                            + (R.rdep("carry") if c > 0 else []), tag="scan"))
                    R.wr(stoks[-1], "cH")
                    R.rd(stoks[-1], "cB")
                    for q, t_ in enumerate(stoks):
                        R.rd(t_, f"cA{q}")
                        if q < C_G - 1:
                            R.rd(t_, "cB")
                    tcar = sca(lambda e, g=g, n=n: e.activation(
                        out=carry.ap()[:, :].rearrange(
                            "p (d s) -> p d s", s=16)[:, C_G * g:C_G * (g + 1),
                                                      n:n + 1],
                        in_=FAr(C_H, C_G * TC).rearrange(
                            "p (d t) -> p d t", t=TC)[:, :, TC - 1:TC],
                        func=ACT.Copy),
                        stoks + R.wdep("carry"), tag="car")
                    R["carry"].w = tcar
                    R.rd(tcar, "cH")
                    accws = R.wdep(f"cacc{g}") if n == 0 else []
                    if n == 0:
                        tac = vec(lambda e, g=g, cr=C_rep: e.tensor_tensor(
                            FAr(C_ACC + g * C_G * TC, C_G * TC).rearrange(
                                "p (d t) -> p d t", t=TC),
                            FAr(C_H, C_G * TC).rearrange(
                                "p (d t) -> p d t", t=TC),
                            cr, ALU.mult), stoks + tcs2 + accws, tag="hc0")
                        R.rd(tac, "cH")
                    else:
                        tmu = vec(lambda e, cr=C_rep: e.tensor_tensor(
                            FAr(C_B, C_G * TC).rearrange(
                                "p (d t) -> p d t", t=TC),
                            FAr(C_H, C_G * TC).rearrange(
                                "p (d t) -> p d t", t=TC),
                            cr, ALU.mult), stoks + tcs2 + R.wdep("cB"),
                            tag="hcm")
                        R.wr(tmu, "cB")
                        R.rd(tmu, "cH")
                        R.rd(tmu, f"cps{bC}", f"cps{bC + 1}")
                        tac = vec(lambda e, g=g: e.tensor_add(
                            FAr(C_ACC + g * C_G * TC, C_G * TC),
                            FAr(C_ACC + g * C_G * TC, C_G * TC),
                            FAr(C_B, C_G * TC)),
                            [tmu] + R.wdep(f"cacc{g}"), tag="hca")
                        R.rd(tac, "cB")
                    R.rd(tac, f"cps{bC}", f"cps{bC + 1}")
                    R.wr(tac, f"cacc{g}")
            tys = []
            for g in range(2):
                tk = dma("gpsimd",
                         cc[f"y_in{l}_{c}"].rearrange(
                             "(d p) t -> p d t", p=128)[:, C_G * g:C_G * (g + 1), :],
                         FAr(C_ACC + g * C_G * TC, C_G * TC).rearrange(
                             "p (d t) -> p d t", t=TC),
                         deps=R.wdep(f"cacc{g}"), tag="yst")
                R.rd(tk, f"cacc{g}")
                tys.append(tk)
            ta = ag(cc[f"y_in{l}_{c}"][:, :],
                    cc[f"y_out{l}_{c}"][:, :], deps=tys, tag="yag")
            R.wr(ta, f"ag_y{l}_{c}")
            agy.append(ta)
        sch.barrier()

    def phaseD(l):
        Lw = LI[l]
        xin = x_ext if l == 0 else x1_s
        xout = x1_s if l == 0 else y_ext
        for kk in range(KDI):
            tk = dma("sync", FAr(D_WO + kk * 1024, 1024),
                     Lw["w_out"][128 * kk:128 * (kk + 1), :],
                     deps=R.wdep(f"dw{kk}"), tag="Dwo")
            R.wr(tk, f"dw{kk}")
        for it in range(NTILE):
            gchunks = [it // 2, 4 + it // 2]
            agdeps = [R[f"ag_y{l}_{cq}"].w for cq in gchunks]
            tyg = []
            for pb in range(KDI):
                msk0 = MSK_YL if pb < NDB else MSK_YU
                prow = (pb % NDB) * 128
                cands = []
                for (s, q) in ((0, 0), (0, 1), (1, 0), (1, 1)):
                    cch = q * 4 + it // 2
                    co2 = (it % 2) * TT
                    cands.append(cc[f"y_out{l}_{cch}"][s * DH + prow:s * DH + prow + 128,
                                                       co2:co2 + TT])
                y0 = FAr(D_T0, TT)
                y1 = FAr(D_T0 + 512, TT)
                tla = dma("sync", y0, cands[0], deps=agdeps + R.wdep("dy0"), tag="yl0")
                tc1 = vec(lambda e, y0=y0, m=mcol(msk0): e.tensor_scalar(
                    y0, y0, m, None, ALU.mult), [tla] + R.rdep("msk"),
                    tag="yc0")
                prev = tc1
                for ci in (1, 2, 3):
                    tlb = dma("sync", y1, cands[ci], deps=agdeps + R.wdep("dy1"),
                              tag="ylx")
                    R.wr(tlb, "dy1")
                    prev = vec(lambda e, y0=y0, y1=y1, m=mcol(msk0 + ci):
                               e.scalar_tensor_tensor(y0, y1, m, y0,
                                                      ALU.mult, ALU.add),
                               [prev, tlb], tag="ycx")
                    R.rd(prev, "dy1")
                R.wr(prev, "dy0")
                ut = FAr(D_T0 + 1024, TT)
                zt = FAr(D_T0 + 1536, TT)
                tu = dma("sync", ut, u_s[128 * pb:128 * (pb + 1), it * TT:(it + 1) * TT],
                         deps=R.wdep("du"), tag="dul")
                R.wr(tu, "du")
                tz2 = dma("sync", zt, z_s[128 * pb:128 * (pb + 1), it * TT:(it + 1) * TT],
                          deps=R.wdep("dz"), tag="dzl")
                R.wr(tz2, "dz")
                ty2 = vec(lambda e, ut=ut, y0=y0, pb=pb: e.scalar_tensor_tensor(
                    y0, ut, ccol(c_dcol + pb), y0, ALU.mult, ALU.add),
                    [prev, tu] + R.rdep("dcol"), tag="yd")
                R.rd(ty2, "du")
                R.wr(ty2, "dy0")
                tsz = sca(lambda e, zt=zt: e.activation(out=zt, in_=zt, func=ACT.Silu),
                          [tz2], tag="dsilu")
                R.wr(tsz, "dz")
                tyg2 = vec(lambda e, pb=pb, y0=y0, zt=zt: e.tensor_mul(
                    FAr(D_YG + pb * TT, TT), y0, zt),
                    [ty2, tsz] + R.wdep(f"dyg{pb}"), tag="ygm")
                R.rd(tyg2, "dy0", "dz")
                R.wr(tyg2, f"dyg{pb}")
                tyg.append(tyg2)
            for mm in range(KDM):
                bank = mm % 4
                grp = mmg(PS(bank, TT),
                          [(FAr(D_WO + kk * 1024 + mm * 128, 128),
                            FAr(D_YG + kk * TT, TT)) for kk in range(KDI)],
                          deps=tyg + [R[f"dw{kk}"].w for kk in range(KDI)]
                          + R.wdep(f"ps{bank}"), tag="omm")
                R.wr(grp, f"ps{bank}")
                for pb in range(KDI):
                    R.rd(grp, f"dyg{pb}")
                so = FAr(D_T0 + 2048, TT)
                if l == 0:
                    txl = dma("sync", FAr(D_T0 + 2560, TT),
                              xn_s[128 * mm:128 * (mm + 1), it * TT:(it + 1) * TT],
                              deps=R.wdep("dxn"), tag="dxnl")
                    R.wr(txl, "dxn")
                    tso = vec(lambda e, bank=bank, so=so: e.tensor_add(
                        so, PS(bank, TT), FAr(D_T0 + 2560, TT)),
                        [grp, txl] + R.wdep("dso"), tag="sadd")
                    R.rd(tso, f"ps{bank}", "dxn")
                else:
                    tso = vec(lambda e, bank=bank, so=so: e.tensor_copy(
                        so, PS(bank, TT)), [grp] + R.wdep("dso"), tag="scp")
                    R.rd(tso, f"ps{bank}")
                R.wr(tso, "dso")
                tgl = dma("sync", FAr(D_T0 + 3072, TT),
                          g_s[128 * mm:128 * (mm + 1), it * TT:(it + 1) * TT],
                          deps=R.wdep("dg"), tag="dgl")
                R.wr(tgl, "dg")
                tmu = vec(lambda e, so=so, b=FAr(D_T0 + 3072, TT): e.tensor_mul(so, so, b),
                          [tso, tgl], tag="sgm")
                R.rd(tmu, "dg")
                R.wr(tmu, "dso")
                txl2 = dma("sync", FAr(D_T0 + 3584, TT),
                           xin[128 * mm:128 * (mm + 1), 3 + it * TT:3 + (it + 1) * TT],
                           deps=R.wdep("dx"), tag="dxl")
                R.wr(txl2, "dx")
                tfin = vec(lambda e, so=so, b=FAr(D_T0 + 3584, TT): e.tensor_add(so, so, b),
                           [tmu, txl2], tag="xadd")
                R.rd(tfin, "dx")
                R.wr(tfin, "dso")
                if l == 0:
                    tk = dma("gpsimd", x1_s[128 * mm:128 * (mm + 1),
                                            3 + it * TT:3 + (it + 1) * TT],
                             so, deps=[tfin], tag="x1st")
                else:
                    tk = dma("gpsimd", y_ext[128 * mm:128 * (mm + 1),
                                             it * TT:(it + 1) * TT],
                             so, deps=[tfin], tag="yout")
                R.rd(tk, "dso")
                R.wr(tk, f"dout{mm}_{it}")
        if l == 0:
            douts = [R[f"dout{mm}_{NTILE - 1}"].w for mm in range(KDM)]
            tki = dma("gpsimd", cc["xt_in"][:, :], x1_s[:, TPC:TPC + 3],
                      deps=douts, tag="xtst")
            ta = ag(cc["xt_in"][:, :], cc["xt_out"][:, :], deps=[tki], tag="xtag")
            for pb in range(KDM):
                h0 = FAr(D_T0, 3)
                h1 = FAr(D_T0 + 8, 3)
                tla = dma("sync", h0, cc["xt_out"][128 * pb:128 * (pb + 1), :],
                          deps=[ta] + R.wdep("dy0"), tag="xtl0")
                tlb = dma("sync", h1, cc["xt_out"][DM + 128 * pb:DM + 128 * (pb + 1), :],
                          deps=[ta] + R.wdep("dy1"), tag="xtl1")
                R.wr(tlb, "dy1")
                t1 = vec(lambda e, h0=h0: e.tensor_scalar(
                    h0, h0, mcol(MSK_XH), None, ALU.mult), [tla] + R.rdep("msk"),
                    tag="xtc0")
                t2 = vec(lambda e, h0=h0, h1=h1: e.scalar_tensor_tensor(
                    h0, h1, mcol(MSK_XH + 1), h0, ALU.mult, ALU.add), [t1, tlb],
                    tag="xtc1")
                R.rd(t2, "dy1")
                R.wr(t2, "dy0")
                tk = dma("gpsimd", x1_s[128 * pb:128 * (pb + 1), 0:3], h0,
                         deps=[t2], tag="xhst")
                R.rd(tk, "dy0")
        sch.barrier()

    for l in range(2):
        phaseAx(l)
        t1 = ag(cc[f"m_in{l}"][:, :], cc[f"m_out{l}"][:, :], deps=sch.barrier(),
                tag="mag")
        R.wr(t1, f"ag_m{l}")
        t2 = ag(cc[f"w_in{l}"][:, :], cc[f"w_out{l}"][:, :], deps=[t1], tag="wag")
        R.wr(t2, f"ag_w{l}")
        t3 = ag(cc[f"bc_in{l}"][:, :], cc[f"bc_out{l}"][:, :], deps=[t2], tag="bcag")
        R.wr(t3, f"ag_bc{l}")
        phaseAz(l)
        use_bphase()
        if l == 0:
            phaseB0()
        else:
            phaseB1()
        use_big()
        phaseC(l)
        phaseD(l)

    sch.barrier()

    NPOOL = 12
    sems = {}
    dma_sems = {}
    semcms = []
    for e in ENGINES:
        cm = nc.semaphore(f"S_{e}")
        sems[e] = cm.__enter__()
        semcms.append(cm)
        dma_sems[e] = []
        for i in range(NPOOL):
            cm2 = nc.semaphore(f"D_{e}{i}")
            dma_sems[e].append(cm2.__enter__())
            semcms.append(cm2)
    blkcm = nc.Block()
    block = blkcm.__enter__()
    sch.finalize(block, sems, dma_sems, NPOOL)
    blkcm.__exit__(None, None, None)
    for cm in reversed(semcms):
        cm.__exit__(None, None, None)
    for cm in reversed(arena["cms"]):
        cm.__exit__(None, None, None)
    for cm in reversed(keep):
        cm.__exit__(None, None, None)
    nops = {e: sch.nseq[e] for e in ENGINES}
    return nc, nops



def _prep_core_inputs(inputs, core):
    p = core % 2
    b = core // 2
    perm = np.arange(DI) if p == 0 else np.concatenate(
        [np.arange(DH, DI), np.arange(0, DH)])
    f = np.float32
    d = {}
    x = np.asarray(inputs["L0_attn_x"], dtype=f)
    tok0 = p * TPC
    halo = np.zeros((3, DM), f) if p == 0 else x[b, tok0 - 3:tok0]
    xc = np.concatenate([halo, x[b, tok0:tok0 + TPC]], axis=0)
    d["x"] = np.ascontiguousarray(xc.T)

    w_in = np.asarray(inputs["L0_attn_W_in"], f)
    d["l0_w_in_x"] = np.ascontiguousarray(w_in[:, :DI][:, perm])
    d["l0_w_in_z"] = np.ascontiguousarray(w_in[:, DI:][:, perm])
    d["l0_conv_w"] = np.ascontiguousarray(
        np.asarray(inputs["L0_attn_conv_w"], f)[:, 0, :][perm])
    d["l0_conv_b"] = np.ascontiguousarray(np.asarray(inputs["L0_attn_conv_b"], f)[perm])
    d["l0_w_xproj"] = np.ascontiguousarray(np.asarray(inputs["L0_attn_W_xproj"], f)[perm])
    d["l0_w_dt"] = np.ascontiguousarray(np.asarray(inputs["L0_attn_W_dt"], f).ravel()[perm])
    d["l0_b_dt"] = np.ascontiguousarray(np.asarray(inputs["L0_attn_b_dt"], f)[perm])
    d["l0_a_log"] = np.ascontiguousarray(np.asarray(inputs["L0_attn_A_log"], f)[perm])
    d["l0_d"] = np.ascontiguousarray(np.asarray(inputs["L0_attn_D"], f)[perm])
    d["l0_w_out"] = np.ascontiguousarray(np.asarray(inputs["L0_attn_W_out"], f)[perm])
    d["l0_norm_w"] = np.ascontiguousarray(np.asarray(inputs["L0_attn_norm_w"], f))
    d["l0_w_gate"] = np.ascontiguousarray(np.asarray(inputs["L0_ffn_w_gate"], f).T)
    d["l0_w_up"] = np.ascontiguousarray(np.asarray(inputs["L0_ffn_w_up"], f).T)
    d["l0_w_down"] = np.ascontiguousarray(np.asarray(inputs["L0_ffn_w_down"], f).T)

    ipw = np.asarray(inputs["L1_attn_in_proj_w"], f)
    d["l1_w_in_x"] = np.ascontiguousarray(ipw[:DI][perm].T)
    d["l1_w_in_z"] = np.ascontiguousarray(ipw[DI:][perm].T)
    d["l1_conv_w"] = np.ascontiguousarray(
        np.asarray(inputs["L1_attn_conv_w"], f)[:, 0, :][perm])
    d["l1_conv_b"] = np.ascontiguousarray(np.asarray(inputs["L1_attn_conv_b"], f)[perm])
    d["l1_w_xproj"] = np.ascontiguousarray(
        np.asarray(inputs["L1_attn_x_proj_w"], f)[:, perm].T)
    d["l1_w_dtproj"] = np.ascontiguousarray(
        np.asarray(inputs["L1_attn_dt_proj_w"], f)[perm].T)
    d["l1_b_dt"] = np.ascontiguousarray(np.asarray(inputs["L1_attn_dt_proj_b"], f)[perm])
    d["l1_a_log"] = np.ascontiguousarray(np.asarray(inputs["L1_attn_A_log"], f)[perm])
    d["l1_d"] = np.ascontiguousarray(np.asarray(inputs["L1_attn_D"], f)[perm])
    d["l1_w_out"] = np.ascontiguousarray(
        np.asarray(inputs["L1_attn_out_proj_w"], f)[:, perm].T)
    d["l1_w1"] = np.ascontiguousarray(np.asarray(inputs["L1_ffn_W1"], f).T)
    d["l1_b1"] = np.ascontiguousarray(np.asarray(inputs["L1_ffn_b1"], f))
    d["l1_w2"] = np.ascontiguousarray(np.asarray(inputs["L1_ffn_W2"], f).T)
    d["l1_b2"] = np.ascontiguousarray(np.asarray(inputs["L1_ffn_b2"], f))

    msk = np.zeros((128, NMSK), f)
    for c in range(NCHUNK):
        owner = c // 4
        if owner == p:
            msk[:, MSK_MW + 3 * c + 0] = 1.0
        elif owner == 0:
            msk[:, MSK_MW + 3 * c + 1] = 1.0
        else:
            msk[:, MSK_MW + 3 * c + 2] = 1.0
    msk[:, MSK_YL + 2 * p + p] = 1.0
    msk[:, MSK_YU + 2 * (1 - p) + p] = 1.0
    if p == 1:
        msk[:, MSK_XH + 0] = 1.0
    d["msk"] = msk
    d["eye"] = np.eye(128, dtype=f)
    seln = np.zeros((32, 32 * 128), f)
    for n in range(32):
        seln[n, n * 128:(n + 1) * 128] = 1.0
    d["seln"] = seln
    return d


_CACHE = {}


def kernel(**inputs):
    if "nc" not in _CACHE:
        nc, nops = build_program()
        _CACHE["nc"] = nc
    nc = _CACHE["nc"]
    in_maps = [_prep_core_inputs(inputs, c) for c in range(8)]
    res = run_bass_kernel_spmd(nc, in_maps, core_ids=list(range(8)))
    out = np.zeros((4, SEQ, DM), dtype=np.float32)
    for c in range(8):
        p = c % 2
        b = c // 2
        out[b, p * TPC:(p + 1) * TPC, :] = res.results[c]["y"].T
    return out
